# version 15
# baseline (speedup 1.0000x reference)
import contextlib
import numpy as np

import concourse.bass as bass
import concourse.mybir as mybir
from concourse.bass_utils import run_bass_kernel_spmd

F32 = mybir.dt.float32
BF16 = mybir.dt.bfloat16
AF = mybir.ActivationFunctionType
ALU = mybir.AluOpType
AX = mybir.AxisListType

B, S, D = 4, 4096, 1024
H, HD, HH = 16, 64, 32
SO = 2048
FF = 4096
NG = 4
EXPB = -20.0
SCL = 0.125
EPS = 1e-5
NW = float(FF * D)
SW = 256


class _S:
    __slots__ = ("h", "n")

    def __init__(self, h):
        self.h = h
        self.n = 0


def build_program():
    nc = bass.Bass("TRN2", target_bir_lowering=False)

    def din(name, shape):
        return nc.dram_tensor(name, shape, F32, kind="ExternalInput").ap()

    d_x = din("x_src", [128, 8 * 8 * 512])
    d_xq = din("xq_src", [128, 4 * 8 * 512])
    d_wq1 = din("wq1", [128, NG * 1024])
    d_wq2 = din("wq2", [128, NG * 1024])
    d_wk1 = din("wk1", [128, NG * 1024])
    d_wk2 = din("wk2", [128, NG * 1024])
    d_wv = din("wv", [128, NG * 8 * 256])
    d_wo = din("wo", [128, 8 * 1024])
    d_w1 = din("w1f", [128, 8 * FF])
    d_w2 = din("w2f", [128, 32 * 1024])
    d_ck = din("cos_k", [128, S])
    d_sk = din("sin_k", [128, S])
    d_cq = din("cos_q", [128, SO])
    d_sq = din("sin_q", [128, SO])
    d_bo = din("b_out", [128, 8])
    d_b1 = din("b1", [128, 32])
    d_b2 = din("b2", [128, 8])
    d_id = din("ident", [128, 128])
    d_out = nc.dram_tensor("outT", [128, 8 * 8 * 256], F32,
                           kind="ExternalOutput").ap()
    d_xc = nc.dram_tensor("x_bf_cache", [128, 12 * 4096], BF16).ap()

    ctx = contextlib.ExitStack()
    with ctx:
        consts = ctx.enter_context(nc.sbuf_tensor("consts", [128, 8], F32))
        ones_b = ctx.enter_context(nc.sbuf_tensor("ones_b", [128, 1], BF16))
        ones_f = ctx.enter_context(nc.sbuf_tensor("ones_f", [128, 128], F32))
        sb_bo = ctx.enter_context(nc.sbuf_tensor("sb_bo", [128, 8], F32))
        sb_b1 = ctx.enter_context(nc.sbuf_tensor("sb_b1", [128, 32], F32))
        sb_b2 = ctx.enter_context(nc.sbuf_tensor("sb_b2", [128, 8], F32))
        thr = ctx.enter_context(nc.sbuf_tensor("thr", [128, 2], F32))
        thrn = ctx.enter_context(nc.sbuf_tensor("thrn", [128, 2], F32))
        wsums = ctx.enter_context(nc.sbuf_tensor("wsums", [128, 128], F32))
        trow = ctx.enter_context(nc.sbuf_tensor("trow", [1, 2], F32))
        x1 = ctx.enter_context(nc.sbuf_tensor("x1", [128, 8 * SO], BF16))

        nc.const_aps.register(F32, EXPB, consts[:, 0:1])
        nc.const_aps.register(F32, SCL, consts[:, 1:2])
        nc.const_aps.register(F32, EPS, consts[:, 2:3])
        nc.const_aps.register(F32, 1.0, consts[:, 3:4])

        names = ["s_init", "s_init2", "s_xt0", "s_xt1", "s_xh0", "s_xh1", "s_gw", "s_w10", "s_w11", "s_w20", "s_w21",
                 "s_rk0", "s_rk1", "s_rk2", "s_rk3", "s_xb0", "s_xb1", "s_res0", "s_res1", "s_wo", "s_st0",
                 "s_st1", "e_dv", "e_ac", "e_gp", "e_pj", "e_pv", "e_sc", "e_av",
                 "e_tp", "e_cm", "e_csm", "e_cbc", "e_dh", "e_do",
                 "e_dsm", "e_dbc"]
        sm = {nm: _S(ctx.enter_context(nc.semaphore(nm))) for nm in names}
        dv = sm["e_dv"]
        ac = sm["e_ac"]

        def mk_vo(v):
            def vo(emit):
                v.wait_ge(dv.h, dv.n)
                emit().then_inc(dv.h, 1)
                dv.n += 1
                return dv.n
            return vo

        def mk_ao(sc):
            def ao(emit):
                sc.wait_ge(ac.h, ac.n)
                emit().then_inc(ac.h, 1)
                ac.n += 1
                return ac.n
            return ao

        def mk_go(gp):
            def go(emit):
                gp.wait_ge(sm["e_gp"].h, sm["e_gp"].n)
                emit().then_inc(sm["e_gp"].h, 1)
                sm["e_gp"].n += 1
                return sm["e_gp"].n
            return go

        with nc.Block() as blk:
            @blk.vector
            def _(v):
                vo = mk_vo(v)
                vo(lambda: v.memset(consts[:, 0:1], EXPB))
                vo(lambda: v.memset(consts[:, 1:2], SCL))
                vo(lambda: v.memset(consts[:, 2:3], EPS))
                vo(lambda: v.memset(consts[:, 3:4], 1.0))
                vo(lambda: v.memset(ones_b[:, :], 1.0))
                vo(lambda: v.memset(ones_f[:, :], 1.0))

            @blk.sync
            def _(s):
                for dst, src in ((sb_bo, d_bo), (sb_b1, d_b1), (sb_b2, d_b2)):
                    s.dma_start(out=dst[:, :], in_=src).then_inc(sm["s_init"].h, 16)
                    sm["s_init"].n += 16

        acx = contextlib.ExitStack()
        cf = contextlib.ExitStack()
        if True:
            attn_o = acx.enter_context(
                nc.sbuf_tensor("attn_o", [128, 8 * SO], BF16, side="right"))
            with contextlib.ExitStack() as at:
                def sbr(name, shape, dt=BF16):
                    return at.enter_context(
                        nc.sbuf_tensor(name, shape, dt, side="right"))

                tab_c = sbr("tab_c", [128, S + SO])
                tab_s = sbr("tab_s", [128, S + SO])
                identb = sbr("identb", [128, 128])
                wg = {nm: sbr("w_" + nm, [128, 1024])
                      for nm in ("q1", "q2", "k1", "k2")}
                wvg = sbr("wvg", [128, 2048])
                xt = [sbr(f"xt{i}", [128, 8 * 512]) for i in range(2)]
                rt1 = [sbr(f"rt1_{i}", [128, 512]) for i in range(4)]
                rt2 = [sbr(f"rt2_{i}", [128, 512]) for i in range(4)]
                rscr = sbr("rscr", [128, 512])
                qsA = sbr("qsA", [128, SO])
                qsB = sbr("qsB", [128, SO])
                ksA = sbr("ksA", [128, S])
                ksB = sbr("ksB", [128, S])
                vtok = sbr("vtok", [128, 32 * 4 * 65])
                slabs = [sbr(f"slab{i}", [128, 2 * 32 * 128]) for i in range(2)]
                nrm = [sbr(f"nrm{i}", [128, 64]) for i in range(4)]
                wstr = [at.enter_context(nc.sbuf_tensor(
                    f"wstr{i}", [128, 512], F32, side="right"))
                    for i in range(2)]
                absd = at.enter_context(nc.sbuf_tensor(
                    "absd", [128, 512], BF16, side="right"))
                rcp = at.enter_context(nc.sbuf_tensor("rcp", [128, 2], F32,
                                                      side="right"))

                p_a = at.enter_context(nc.psum_tensor("p_a", [128, 1024], F32))
                p_b = at.enter_context(nc.psum_tensor("p_b", [128, 1024], F32))
                p_m = at.enter_context(nc.psum_tensor("p_m", [128, 1024], F32))
                p_ts = [at.enter_context(nc.psum_tensor(f"p_t{i}", [128, 256],
                                                        BF16)) for i in range(2)]
                PQ = [p_a, p_b]

                with nc.Block() as blk:
                    @blk.gpsimd
                    def _(g):
                        for dst, src, off, w in ((tab_c, d_ck, 0, S),
                                                 (tab_s, d_sk, 0, S),
                                                 (tab_c, d_cq, S, SO),
                                                 (tab_s, d_sq, S, SO)):
                            g.dma_start(out=dst[:, off:off + w], in_=src
                                        ).then_inc(sm["s_init2"].h, 16)
                            sm["s_init2"].n += 16
                        g.dma_start(out=identb[:, :], in_=d_id
                                    ).then_inc(sm["s_init2"].h, 16)
                        sm["s_init2"].n += 16

                initN = sm["s_init"].n
                init2N = sm["s_init2"].n

                with nc.Block() as blk:
                  for g in range(NG):
                    pj0 = sm["e_pj"].n
                    pv0 = sm["e_pv"].n
                    xt0 = [sm["s_xt0"].n, sm["s_xt1"].n]
                    xh0 = [sm["s_xh0"].n, sm["s_xh1"].n]
                    gw0 = sm["s_gw"].n
                    rk0 = [sm[f"s_rk{i}"].n for i in range(4)]
                    dvP = dv.n
                    acP = ac.n
                    w1b = [sm["s_w10"].n, sm["s_w11"].n]
                    def rop(rt, j):
                        if rt < 4:
                            return dvP + 1 + 6 * rt + j
                        return dvP + 25 + (rt - 4) * 10 + j

                    def vev(vi):
                        return dvP + 25 + (vi // 4) * 10 + 6 + (vi % 4) + 1

                    if True:
                        @blk.gpsimd
                        def _(gp):
                            if g > 0:
                                gp.wait_ge(sm["e_pj"].h, pj0)
                                gp.wait_ge(sm["e_pv"].h, pv0)
                            for nm, src in (("q1", d_wq1), ("q2", d_wq2),
                                            ("k1", d_wk1), ("k2", d_wk2)):
                                gp.dma_start(out=wg[nm][:, :],
                                             in_=src[:, g * 1024:(g + 1) * 1024]
                                             ).then_inc(sm["s_gw"].h, 16)
                                sm["s_gw"].n += 16
                            gp.dma_start(out=wvg[:, :],
                                         in_=d_wv[:, g * 2048:(g + 1) * 2048]
                                         ).then_inc(sm["s_gw"].h, 16)
                            sm["s_gw"].n += 16
                            if g == 0:
                                for i in range(12):
                                    if 2 <= i < 6:
                                        gp.wait_ge(sm["e_pj"].h, pj0 + i - 1)
                                    elif i >= 6:
                                        gp.wait_ge(sm["e_pv"].h,
                                                   pv0 + 4 * (i - 2 - 4) + 4)
                                    if i >= 2:
                                        gp.wait_ge(sm[f"s_xb{i % 2}"].h,
                                                   16 * (i // 2))
                                    src = (d_xq[:, i * 4096:(i + 1) * 4096]
                                           if i < 4 else
                                           d_x[:, (i - 4) * 4096:
                                               (i - 3) * 4096])
                                    gp.dma_start(out=xt[i % 2][:, :], in_=src
                                                 ).then_inc(
                                                     sm[f"s_xt{i % 2}"].h, 16)
                                    sm[f"s_xt{i % 2}"].n += 16
                                    gp.wait_ge(sm[f"s_xt{i % 2}"].h,
                                               xt0[i % 2] + 16 * (i // 2 + 1))
                                    gp.dma_start(
                                        out=d_xc[:, i * 4096:(i + 1) * 4096],
                                        in_=xt[i % 2][:, :]
                                        ).then_inc(sm[f"s_xb{i % 2}"].h, 16)
                                    sm[f"s_xb{i % 2}"].n += 16
                            for i in range(32):
                                gi = g * 32 + i
                                if i >= 2:
                                    gp.wait_ge(ac.h, acP + 256 + i - 1)
                                elif g > 0:
                                    gp.wait_ge(ac.h, acP)
                                src_ = (d_w1[:, gi * 512:(gi + 1) * 512]
                                        if gi < 64 else
                                        d_w2[:, (gi - 64) * 512:
                                             (gi - 63) * 512])
                                gp.dma_start(out=wstr[i % 2][:, :], in_=src_
                                             ).then_inc(sm[f"s_w1{i % 2}"].h, 16)
                                sm[f"s_w1{i % 2}"].n += 16

                        @blk.tensor
                        def _(t):
                            t.wait_ge(sm["s_gw"].h, gw0 + 80)
                            for rt in range(12):
                                if g == 0:
                                    t.wait_ge(sm[f"s_xt{rt % 2}"].h,
                                              xt0[rt % 2] + 16 * (rt // 2 + 1))
                                else:
                                    t.wait_ge(sm[f"s_xh{rt % 2}"].h,
                                              xh0[rt % 2] + 16 * (rt // 2 + 1))
                                if rt >= 2:
                                    t.wait_ge(dv.h, rop(rt - 2, 5))
                                elif g > 0:
                                    t.wait_ge(ac.h, acP - 32)
                                w1_, w2_ = (("q1", "q2") if rt < 4 else
                                            ("k1", "k2"))
                                pq = PQ[rt % 2]
                                for dc in range(8):
                                    t.matmul(pq[:, 0:512],
                                             wg[w1_][:, dc * 128:(dc + 1) * 128],
                                             xt[rt % 2][:, dc * 512:(dc + 1) * 512],
                                             start=(dc == 0), stop=(dc == 7))
                                for dc in range(8):
                                    ins = t.matmul(
                                        pq[:, 512:1024],
                                        wg[w2_][:, dc * 128:(dc + 1) * 128],
                                        xt[rt % 2][:, dc * 512:(dc + 1) * 512],
                                        start=(dc == 0), stop=(dc == 7))
                                ins.then_inc(sm["e_pj"].h, 1)
                                sm["e_pj"].n += 1
                                if rt < 4:
                                    continue
                                for sub in range(4):
                                    vi = (rt - 4) * 4 + sub
                                    if vi >= 2:
                                        t.wait_ge(dv.h, vev(vi - 2))
                                    elif g > 0:
                                        t.wait_ge(dv.h, dv_att_end)
                                    pv = (p_m[:, 0:256] if vi % 2 == 0 else
                                          p_m[:, 512:768])
                                    for dc in range(8):
                                        ins = t.matmul(
                                            pv,
                                            xt[rt % 2][:, dc * 512 + sub * 128:
                                                       dc * 512 + sub * 128
                                                       + 128],
                                            wvg[:, dc * 256:(dc + 1) * 256],
                                            start=(dc == 0), stop=(dc == 7))
                                    ins.then_inc(sm["e_pv"].h, 1)
                                    sm["e_pv"].n += 1

                        @blk.vector
                        def _(v):
                            vo = mk_vo(v)
                            if g == 0:
                                v.wait_ge(sm["s_init"].h, initN)
                                v.wait_ge(sm["s_init2"].h, init2N)
                            ones_view = vtok[:, :].rearrange(
                                "p (t h c) -> p t h c", t=32, h=4)[:, :, :, 64:65]
                            vo(lambda: v.memset(ones_view, 1.0))
                            for rt in range(12):
                                v.wait_ge(sm["e_pj"].h, pj0 + rt + 1)
                                if rt >= 4:
                                    v.wait_ge(sm[f"s_rk{rt % 4}"].h,
                                              rk0[rt % 4] + 128 * (rt // 4))
                                elif g > 0:
                                    v.wait_ge(sm[f"s_rk{rt % 4}"].h,
                                              rk0[rt % 4])
                                pq = PQ[rt % 2]
                                toff = ((S + rt * 512) if rt < 4 else
                                        ((rt - 4) * 512))
                                tc = tab_c[:, toff:toff + 512]
                                tsn = tab_s[:, toff:toff + 512]
                                r1, r2 = rt1[rt % 4], rt2[rt % 4]
                                vo(lambda: v.tensor_mul(r1[:, :], pq[:, 0:512],
                                                        tc))
                                vo(lambda: v.tensor_mul(rscr[:, :],
                                                        pq[:, 512:1024], tsn))
                                vo(lambda: v.tensor_sub(r1[:, :], r1[:, :],
                                                        rscr[:, :]))
                                vo(lambda: v.tensor_mul(r2[:, :], pq[:, 0:512],
                                                        tsn))
                                vo(lambda: v.tensor_mul(rscr[:, :],
                                                        pq[:, 512:1024], tc))
                                vo(lambda: v.tensor_add(r2[:, :], r2[:, :],
                                                        rscr[:, :]))
                                if rt >= 4:
                                    for sub in range(4):
                                        vi = (rt - 4) * 4 + sub
                                        v.wait_ge(sm["e_pv"].h, pv0 + vi + 1)
                                        pv = (p_m[:, 0:256] if vi % 2 == 0 else
                                              p_m[:, 512:768])
                                        dst = vtok[:, :].rearrange(
                                            "p (t h c) -> p t h c", t=32, h=4
                                            )[:, vi:vi + 1, :, 0:64]
                                        vo(lambda: v.tensor_copy(dst, pv))
                            assert dv.n == dvP + 1 + 24 + 8 * 10

                        def emit_restack(eng, rts):
                            for rt in rts:
                                eng.wait_ge(dv.h, rop(rt, 6))
                                r1, r2 = rt1[rt % 4], rt2[rt % 4]
                                if rt < 4:
                                    dsts = (qsA, qsB)
                                    off = rt * 512
                                else:
                                    dsts = (ksA, ksB)
                                    off = (rt - 4) * 512
                                for half, dstT in enumerate(dsts):
                                    b0 = half * 64
                                    for hh, srcT in ((0, r1), (32, r2)):
                                        eng.dma_start(
                                            out=dstT[hh:hh + 32, off:off + 512],
                                            in_=srcT[b0:b0 + 32, :]
                                            ).then_inc(sm[f"s_rk{rt % 4}"].h, 16)
                                        sm[f"s_rk{rt % 4}"].n += 16
                                        eng.dma_start(
                                            out=dstT[64 + hh:96 + hh,
                                                     off:off + 512],
                                            in_=srcT[b0 + 32:b0 + 64, :]
                                            ).then_inc(sm[f"s_rk{rt % 4}"].h, 16)
                                        sm[f"s_rk{rt % 4}"].n += 16

                        def emit_xload(eng, i):
                            if 2 <= i < 6:
                                eng.wait_ge(sm["e_pj"].h, pj0 + i - 1)
                            elif i >= 6:
                                eng.wait_ge(sm["e_pv"].h,
                                            pv0 + 4 * (i - 2 - 4) + 4)
                            else:
                                eng.wait_ge(sm["e_pv"].h, pv0)
                            if g == 1:
                                eng.wait_ge(sm[f"s_xb{i % 2}"].h,
                                            sm[f"s_xb{i % 2}"].n)
                            eng.dma_start(
                                out=xt[i % 2][:, :],
                                in_=d_xc[:, i * 4096:(i + 1) * 4096]
                                ).then_inc(sm[f"s_xh{i % 2}"].h, 16)
                            sm[f"s_xh{i % 2}"].n += 16

                        def emit_queue(eng, par):
                            if g > 0:
                                eng.wait_ge(sm["e_av"].h, sm["e_av"].n)
                                for i in range(par, 6, 2):
                                    emit_xload(eng, i)
                                for r in range(par, 6, 2):
                                    emit_restack(eng, (r,))
                                    emit_xload(eng, r + 6)
                                for r in range(par + 6, 12, 2):
                                    emit_restack(eng, (r,))
                            else:
                                emit_restack(eng, range(par, 12, 2))

                        @blk.scalar
                        def _(sc):
                            emit_queue(sc, 1)

                        @blk.sync
                        def _(s):
                            emit_queue(s, 0)

                    sc0 = sm["e_sc"].n
                    av0 = sm["e_av"].n
                    tp0 = sm["e_tp"].n
                    ac0 = ac.n
                    dvA = dv.n
                    nm2 = lambda u, hb: dvA + u * 6 + 2 * (hb + 1)
                    nrm_done = lambda u: dvA + u * 6 + 4
                    ev_done = lambda u: dvA + u * 6 + 6
                    rkN = [sm[f"s_rk{i}"].n for i in range(4)]
                    if True:
                        @blk.tensor
                        def _(t):
                            for i in range(4):
                                t.wait_ge(sm[f"s_rk{i}"].h, rkN[i])
                            for w in range(34):
                                if 1 <= w <= 32 and w - 1 >= 2:
                                    t.wait_ge(dv.h, nrm_done(w - 3))
                                for ttg in range(8):
                                    if w < 32:
                                        u, pr, sg = w, w % 2, w // 2
                                        qs = qsA if pr == 0 else qsB
                                        ks = ksA if pr == 0 else ksB
                                        j = u * 8 + ttg
                                        if j >= 2:
                                            t.wait_ge(ac.h, ac0 + j - 1)
                                        else:
                                            t.wait_ge(dv.h, rop(10 + j, 5))
                                        pq = PQ[j % 2]
                                        for tq in range(4):
                                            tt = ttg * 4 + tq
                                            for hb in range(2):
                                                ins = t.matmul(
                                                    pq[:, hb * 512 + tq * 128:
                                                       hb * 512 + tq * 128 + 128],
                                                    ks[hb * 64:hb * 64 + 64,
                                                       tt * 128:(tt + 1) * 128],
                                                    qs[hb * 64:hb * 64 + 64,
                                                       sg * 128:(sg + 1) * 128],
                                                    start=(tq == 0),
                                                    stop=(tq == 3))
                                        ins.then_inc(sm["e_sc"].h, 1)
                                        sm["e_sc"].n += 1
                                    if 1 <= w <= 32:
                                        u = w - 1
                                        pr = u % 2
                                        if w == 1 and ttg == 0:
                                            t.wait_ge(dv.h, vev(31))
                                        t.wait_ge(ac.h,
                                                  ac0 + u * 8 + ttg + 1)
                                        pav = (p_m[:, 0:130] if u % 2 == 0 else
                                               p_m[:, 512:642])
                                        for hb in range(2):
                                            for tq in range(4):
                                                tt = ttg * 4 + tq
                                                ins = t.matmul(
                                                    pav[:, hb * 65:hb * 65 + 65],
                                                    slabs[pr][:,
                                                              ttg * 1024 +
                                                              hb * 512 +
                                                              tq * 128:
                                                              ttg * 1024 +
                                                              hb * 512 +
                                                              tq * 128 + 128],
                                                    vtok[:, tt * 260 +
                                                         (pr * 2 + hb) * 65:
                                                         tt * 260 +
                                                         (pr * 2 + hb) * 65
                                                         + 65],
                                                    start=(ttg == 0 and
                                                           hb == 0 and tq == 0),
                                                    stop=(ttg == 7 and
                                                          hb == 1 and tq == 3))
                                        ins.then_inc(sm["e_av"].h, 1)
                                        sm["e_av"].n += 1
                                if 2 <= w:
                                    u = w - 2
                                    if u < 32:
                                        pr, sg = u % 2, u // 2
                                        if u >= 4:
                                            t.wait_ge(dv.h, ev_done(u - 2))
                                        for hb in range(2):
                                            hg = 4 * g + pr * 2 + hb
                                            t.wait_ge(dv.h, nm2(u, hb))
                                            bp = 64 * (hg % 2)
                                            ins = t.transpose(
                                                p_ts[u % 2][bp:bp + 64,
                                                            hb * 128:
                                                            hb * 128 + 128],
                                                nrm[(u % 2) * 2 + hb][:, 0:64],
                                                identb[:, :])
                                        ins.then_inc(sm["e_tp"].h, 1)
                                        sm["e_tp"].n += 1

                        @blk.scalar
                        def _(sc):
                            ao = mk_ao(sc)
                            for u in range(32):
                                pr = u % 2
                                if u >= 2:
                                    sc.wait_ge(sm["e_av"].h,
                                               av0 + (u - 2) * 8 + 8)
                                elif g > 0 and u == 0:
                                    sc.wait_ge(sm["e_av"].h, av0)
                                for ttg in range(8):
                                    j = u * 8 + ttg
                                    if ttg % 2 == 0:
                                        sc.wait_ge(sm["e_sc"].h, sc0 + j + 2)
                                    pq = PQ[j % 2]
                                    dst = slabs[pr][:, ttg * 1024:
                                                     ttg * 1024 + 1024]
                                    sc.activation(
                                        dst, pq[:, 0:1024], AF.Exp,
                                        bias=EXPB, scale=SCL
                                        ).then_inc(ac.h, 1)
                                    ac.n += 1
                            assert ac.n == ac0 + 256
                            for i in range(32):
                                gi = g * 32 + i
                                sc.wait_ge(sm[f"s_w1{i % 2}"].h,
                                           w1b[i % 2] + 16 * (i // 2 + 1))
                                ao(lambda: sc.activation(
                                    absd[:, :], wstr[i % 2][:, :], AF.Abs,
                                    accum_out=wsums[:, gi:gi + 1]))
                            assert ac.n == ac0 + 288

                        @blk.vector
                        def _(v):
                            vo = mk_vo(v)
                            for u in range(32):
                                pr, sg = u % 2, u // 2
                                v.wait_ge(sm["e_av"].h, av0 + (u + 1) * 8)
                                pav = (p_m[:, 0:130] if u % 2 == 0 else
                                       p_m[:, 512:642])
                                for hb in range(2):
                                    vo(lambda: v.reciprocal(
                                        rcp[:, 0:1],
                                        pav[:, hb * 65 + 64:hb * 65 + 65]))
                                    vo(lambda: v.tensor_scalar(
                                        nrm[(u % 2) * 2 + hb][:, 0:64],
                                        pav[:, hb * 65:hb * 65 + 64],
                                        rcp[:, 0:1], None, op0=ALU.mult))
                                v.wait_ge(sm["e_tp"].h, tp0 + u + 1)
                                for hb in range(2):
                                    hg = 4 * g + pr * 2 + hb
                                    bp = 64 * (hg % 2)
                                    dco = hg // 2
                                    col = dco * SO + sg * 128
                                    vo(lambda: v.tensor_copy(
                                        attn_o[bp:bp + 64, col:col + 128],
                                        p_ts[u % 2][bp:bp + 64,
                                                    hb * 128:hb * 128 + 128]))
                            assert dv.n == dvA + 32 * 6
                        dv_att_end = dv.n

            w1t = cf.enter_context(nc.sbuf_tensor("w1t", [128, 8 * FF], BF16))
            wst2 = [cf.enter_context(nc.sbuf_tensor(f"wst2_{i}", [128, 512],
                                                    F32)) for i in range(4)]
            w4 = lambda i: f"s_w{1 + (i % 4) // 2}{i % 2}"
            m1b = cf.enter_context(nc.sbuf_tensor("m1b", [128, 512], BF16))
            with contextlib.ExitStack() as cx:
                def sbc(name, shape, dt, n=None):
                    if n is None:
                        return cx.enter_context(
                            nc.sbuf_tensor(name, shape, dt, side="right"))
                    return [cx.enter_context(
                        nc.sbuf_tensor(f"{name}{i}", shape, dt, side="right"))
                        for i in range(n)]

                wo_sb = sbc("wo_sb", [128, 8192], BF16)
                res1 = sbc("res1", [128, 4096], F32)
                res1b = sbc("res1b", [128, 4096], BF16)
                sqb = sbc("sqb", [128, 4096], BF16)
                xres = sbc("xres", [128, 512], F32, 2)
                scr = sbc("scr", [128, 512], F32)
                rw_m = sbc("rw_m", [1, 512], F32)
                rw_s = sbc("rw_s", [1, 512], F32)
                rw_t = sbc("rw_t", [1, 512], F32)
                rw_d = sbc("rw_d", [1, 512], F32)
                rw_r = sbc("rw_r", [1, 512], F32)
                p_p = [cx.enter_context(nc.psum_tensor(f"p_p{i}", [128, 512],
                                                       F32)) for i in range(2)]
                p_me = cx.enter_context(nc.psum_tensor("p_me", [1, 512], F32))
                p_sq = cx.enter_context(nc.psum_tensor("p_sq", [1, 512], F32))
                p_bc = cx.enter_context(nc.psum_tensor("p_bc", [128, 1024], F32))

                cm0 = sm["e_cm"].n
                csm0 = sm["e_csm"].n
                cbc0 = sm["e_cbc"].n
                dvC = dv.n
                acC = ac.n
                cev = lambda i: dvC + 6 + (i // 8) * 77 + (i % 8) * 7 + 2
                csq = lambda i: dvC + 6 + (i // 8) * 77 + (i % 8) * 7 + 3
                ctn = lambda t: (dvC + 6 + ((t // 2) // 8) * 77 +
                                 ((t // 2) % 8) * 7 + (5 if t % 2 == 0 else 7))
                crw = lambda st: dvC + 6 + st * 77 + 60
                crc = lambda st: dvC + 6 + st * 77 + 61
                cx1 = lambda st: dvC + 6 + st * 77 + 77
                csd = lambda st: acC + st + 1
                res0 = [sm["s_res0"].n, sm["s_res1"].n]

                w1c = [sm["s_w10"].n, sm["s_w11"].n,
                       sm["s_w20"].n, sm["s_w21"].n]
                with nc.Block() as blk:
                    @blk.gpsimd
                    def _(gp):
                        gp.dma_start(out=wo_sb[:, :], in_=d_wo
                                     ).then_inc(sm["s_wo"].h, 16)
                        sm["s_wo"].n += 16
                        for i in range(64):
                            if i >= 4:
                                gp.wait_ge(dv.h, ctn(i - 4))
                            gp.dma_start(
                                out=wst2[i % 4][:, :],
                                in_=d_w1[:, i * 512:(i + 1) * 512]
                                ).then_inc(sm[w4(i)].h, 16)
                            sm[w4(i)].n += 16

                    @blk.sync
                    def _(s):
                        for i in range(32):
                            st, ec = i // 8, i % 8
                            if i >= 2:
                                s.wait_ge(dv.h, cev(i - 2) - 1)
                            s.dma_start(out=xres[i % 2][:, :],
                                        in_=d_xq[:, st * 4096 + ec * 512:
                                                 st * 4096 + ec * 512 + 512]
                                        ).then_inc(sm[f"s_res{i % 2}"].h, 16)
                            sm[f"s_res{i % 2}"].n += 16

                    @blk.tensor
                    def _(t):
                        t.matmul(p_me[0:1, 0:128], ones_f[:, 0:1],
                                 wsums[:, 0:128],
                                 start=True, stop=True).then_inc(sm["e_csm"].h, 1)
                        sm["e_csm"].n += 1
                        t.wait_ge(dv.h, dvC + 4)
                        t.matmul(p_bc[:, 0:2], ones_f[0:1, 0:128],
                                 trow[0:1, 0:2], start=True,
                                 stop=True).then_inc(sm["e_cbc"].h, 1)
                        sm["e_cbc"].n += 1
                        t.wait_ge(sm["s_wo"].h, sm["s_wo"].n)
                        for st in range(4):
                            if st >= 1:
                                t.wait_ge(dv.h, cx1(st - 1))
                            for ec in range(8):
                                i = st * 8 + ec
                                if i >= 2:
                                    t.wait_ge(dv.h, cev(i - 2))
                                for dc in range(8):
                                    ins = t.matmul(
                                        p_p[i % 2][:, 0:512],
                                        wo_sb[:, dc * 1024 + ec * 128:
                                              dc * 1024 + ec * 128 + 128],
                                        attn_o[:, dc * SO + st * 512:
                                               dc * SO + st * 512 + 512],
                                        start=(dc == 0), stop=(dc == 7))
                                ins.then_inc(sm["e_cm"].h, 1)
                                sm["e_cm"].n += 1
                            for ec in range(8):
                                t.wait_ge(dv.h, cev(st * 8 + ec))
                                t.matmul(p_me[0:1, :], ones_b[:, 0:1],
                                         res1b[:, ec * 512:(ec + 1) * 512],
                                         start=(ec == 0), stop=(ec == 7))
                            for ec in range(8):
                                t.wait_ge(dv.h, csq(st * 8 + ec))
                                ins = t.matmul(p_sq[0:1, :], ones_b[:, 0:1],
                                               sqb[:, ec * 512:(ec + 1) * 512],
                                               start=(ec == 0), stop=(ec == 7))
                            ins.then_inc(sm["e_csm"].h, 1)
                            sm["e_csm"].n += 1
                            t.wait_ge(dv.h, crc(st))
                            t.matmul(p_bc[:, 0:512], ones_f[0:1, 0:128],
                                     rw_m[0:1, :], start=True, stop=True)
                            t.matmul(p_bc[:, 512:1024], ones_f[0:1, 0:128],
                                     rw_r[0:1, :], start=True,
                                     stop=True).then_inc(sm["e_cbc"].h, 1)
                            sm["e_cbc"].n += 1

                    @blk.vector
                    def _(v):
                        vo = mk_vo(v)
                        v.wait_ge(sm["e_csm"].h, csm0 + 1)
                        vo(lambda: v.tensor_reduce(trow[0:1, 0:1],
                                                   p_me[0:1, 0:64], AX.X,
                                                   ALU.add))
                        vo(lambda: v.tensor_reduce(trow[0:1, 1:2],
                                                   p_me[0:1, 64:128], AX.X,
                                                   ALU.add))
                        vo(lambda: v.tensor_scalar(trow[0:1, 0:2],
                                                   trow[0:1, 0:2], 0.5 / NW,
                                                   None, op0=ALU.mult))
                        vo(lambda: v.tensor_scalar(trow[0:1, 0:2],
                                                   trow[0:1, 0:2], 0.5e-8,
                                                   None, op0=ALU.max))
                        v.wait_ge(sm["e_cbc"].h, cbc0 + 1)
                        vo(lambda: v.tensor_copy(thr[:, :], p_bc[:, 0:2]))
                        vo(lambda: v.tensor_scalar(thrn[:, :], p_bc[:, 0:2],
                                                   -1.0, None, op0=ALU.mult))
                        for st in range(4):
                            for ec in range(8):
                                i = st * 8 + ec
                                v.wait_ge(sm["e_cm"].h, cm0 + i + 1)
                                v.wait_ge(sm[f"s_res{i % 2}"].h,
                                          res0[i % 2] + 16 * (i // 2 + 1))
                                vo(lambda: v.scalar_tensor_tensor(
                                    res1[:, ec * 512:(ec + 1) * 512],
                                    p_p[i % 2][:, 0:512], sb_bo[:, ec:ec + 1],
                                    xres[i % 2][:, :],
                                    op0=ALU.add, op1=ALU.add))
                                vo(lambda: v.tensor_copy(
                                    res1b[:, ec * 512:(ec + 1) * 512],
                                    res1[:, ec * 512:(ec + 1) * 512]))
                                vo(lambda: v.tensor_mul(
                                    sqb[:, ec * 512:(ec + 1) * 512],
                                    res1b[:, ec * 512:(ec + 1) * 512],
                                    res1b[:, ec * 512:(ec + 1) * 512]))
                                for tt_ in (2 * i, 2 * i + 1):
                                    v.wait_ge(sm[w4(tt_)].h,
                                              w1c[tt_ % 4] +
                                              16 * (tt_ // 4 + 1))
                                    vo(lambda: v.tensor_scalar(
                                        m1b[:, :], wst2[tt_ % 4][:, :],
                                        thrn[:, 0:1], None, op0=ALU.is_le))
                                    vo(lambda: v.scalar_tensor_tensor(
                                        w1t[:, tt_ * 512:(tt_ + 1) * 512],
                                        wst2[tt_ % 4][:, :], thr[:, 0:1],
                                        m1b[:, :], op0=ALU.is_ge,
                                        op1=ALU.subtract))
                            v.wait_ge(sm["e_csm"].h, csm0 + st + 2)
                            vo(lambda: v.tensor_scalar(rw_m[0:1, :],
                                                       p_me[0:1, :],
                                                       1.0 / 1024.0, None,
                                                       op0=ALU.mult))
                            vo(lambda: v.tensor_scalar(rw_s[0:1, :],
                                                       p_sq[0:1, :],
                                                       1.0 / 1024.0, None,
                                                       op0=ALU.mult))
                            vo(lambda: v.tensor_mul(rw_t[0:1, :], rw_m[0:1, :],
                                                    rw_m[0:1, :]))
                            vo(lambda: v.tensor_sub(rw_s[0:1, :], rw_s[0:1, :],
                                                    rw_t[0:1, :]))
                            v.wait_ge(ac.h, csd(st))
                            vo(lambda: v.reciprocal(rw_r[0:1, :], rw_d[0:1, :]))
                            v.wait_ge(sm["e_cbc"].h, cbc0 + st + 2)
                            for ec in range(8):
                                vo(lambda: v.tensor_sub(
                                    scr[:, :],
                                    res1[:, ec * 512:(ec + 1) * 512],
                                    p_bc[:, 0:512]))
                                vo(lambda: v.tensor_mul(
                                    x1[:, ec * SO + st * 512:
                                       ec * SO + st * 512 + 512],
                                    scr[:, :], p_bc[:, 512:1024]))
                        assert dv.n == dvC + 6 + 4 * 77

                    @blk.scalar
                    def _(sc):
                        ao = mk_ao(sc)
                        for st in range(4):
                            sc.wait_ge(dv.h, crw(st))
                            ao(lambda: sc.activation(rw_d[0:1, :], rw_s[0:1, :],
                                                     AF.Sqrt, bias=EPS,
                                                     scale=1.0))
                        assert ac.n == acC + 4

        acx.close()

        with contextlib.ExitStack() as ds:
            def sbd(name, shape, dt, n=None):
                if n is None:
                    return ds.enter_context(
                        nc.sbuf_tensor(name, shape, dt, side="right"))
                return [ds.enter_context(
                    nc.sbuf_tensor(f"{name}{i}", shape, dt, side="right"))
                    for i in range(n)]

            w2t = sbd("w2t", [128, 32 * 1024], BF16)
            h_sb = sbd("h_sb", [128, 32 * 256], BF16)
            res2 = sbd("res2", [128, 8 * 256], F32)
            sq2 = sbd("sq2_", [128, 256], BF16, 2)
            outt = sbd("outt", [128, 256], F32, 2)
            dw_m = sbd("dw_m", [1, 256], F32)
            dw_s = sbd("dw_s", [1, 256], F32)
            dw_t = sbd("dw_t", [1, 256], F32)
            dw_d = sbd("dw_d", [1, 256], F32)
            dw_r = sbd("dw_r", [1, 256], F32)
            p_h = ds.enter_context(nc.psum_tensor("p_h", [128, 1024], F32))
            p_o = ds.enter_context(nc.psum_tensor("p_o", [128, 1024], F32))
            p_m2 = ds.enter_context(nc.psum_tensor("p_m2", [1, 256], F32))
            p_q2 = ds.enter_context(nc.psum_tensor("p_q2", [1, 256], F32))
            p_b2 = ds.enter_context(nc.psum_tensor("p_b2", [128, 512], F32))

            dh0 = sm["e_dh"].n
            do0 = sm["e_do"].n
            dsm0 = sm["e_dsm"].n
            dbc0 = sm["e_dbc"].n
            dvF = dv.n
            acF = ac.n
            w2c = [sm["s_w10"].n, sm["s_w11"].n,
                   sm["s_w20"].n, sm["s_w21"].n]
            gpT = sm["e_gp"].n
            w20 = [sm["s_w20"].n, sm["s_w21"].n]
            dev = lambda i: dvF + 128 + (i // 8) * 37 + (i % 8) * 2 + 1
            dsq = lambda i: dvF + 128 + (i // 8) * 37 + (i % 8) * 2 + 2
            drw = lambda st: dvF + 128 + st * 37 + 20
            drc = lambda st: dvF + 128 + st * 37 + 21
            dout = lambda i: dvF + 128 + (i // 8) * 37 + 21 + 2 * ((i % 8) + 1)
            dall = lambda st: dvF + 128 + st * 37 + 37
            dgel = lambda i: acF + (i // 32) * 33 + (i % 32) + 1
            dsqr = lambda st: acF + st * 33 + 33
            st0 = [sm["s_st0"].n, sm["s_st1"].n]
            with nc.Block() as blk:
                @blk.tensor
                def _(t):
                    for st in range(8):
                        for fc in range(32):
                            i = st * 32 + fc
                            if i >= 2:
                                t.wait_ge(ac.h, dgel(i - 2))
                            ph = (p_h[:, 0:256] if fc % 2 == 0 else
                                  p_h[:, 512:768])
                            for dc in range(8):
                                ins = t.matmul(
                                    ph,
                                    w1t[:, dc * FF + fc * 128:
                                        dc * FF + fc * 128 + 128],
                                    x1[:, dc * SO + st * 256:
                                       dc * SO + st * 256 + 256],
                                    start=(dc == 0), stop=(dc == 7))
                            ins.then_inc(sm["e_dh"].h, 1)
                            sm["e_dh"].n += 1
                        t.wait_ge(ac.h, dgel(st * 32 + 31))

                        def stats(dc):
                            if st >= 1 and dc == 0:
                                t.wait_ge(dv.h, drw(st - 1))
                            t.wait_ge(dv.h, dev(st * 8 + dc))
                            t.matmul(p_m2[0:1, :], ones_f[:, 0:1],
                                     res2[:, dc * 256:(dc + 1) * 256],
                                     start=(dc == 0), stop=(dc == 7))
                            t.wait_ge(dv.h, dsq(st * 8 + dc))
                            ins = t.matmul(p_q2[0:1, :], ones_b[:, 0:1],
                                           sq2[dc % 2][:, :],
                                           start=(dc == 0), stop=(dc == 7))
                            if dc == 7:
                                ins.then_inc(sm["e_dsm"].h, 1)
                                sm["e_dsm"].n += 1

                        for dc in range(8):
                            i = st * 8 + dc
                            if st == 0:
                                t.wait_ge(dv.h, dvF + (dc // 4 + 1) * 64)
                            if i >= 2:
                                t.wait_ge(dv.h, dev(i - 2))
                            po = (p_o[:, 0:256] if dc % 2 == 0 else
                                  p_o[:, 512:768])
                            for fc in range(32):
                                ins = t.matmul(
                                    po,
                                    w2t[:, fc * 1024 + dc * 128:
                                        fc * 1024 + dc * 128 + 128],
                                    h_sb[:, fc * 256:(fc + 1) * 256],
                                    start=(fc == 0), stop=(fc == 31))
                            ins.then_inc(sm["e_do"].h, 1)
                            sm["e_do"].n += 1
                            if dc >= 1:
                                stats(dc - 1)
                        stats(7)
                        t.wait_ge(dv.h, drc(st))
                        if st >= 1:
                            t.wait_ge(dv.h, dall(st - 1))
                        t.matmul(p_b2[:, 0:256], ones_f[0:1, 0:128],
                                 dw_m[0:1, :], start=True, stop=False)
                        t.matmul(p_b2[:, 256:512], ones_f[0:1, 0:128],
                                 dw_r[0:1, :], start=False, stop=True
                                 ).then_inc(sm["e_dbc"].h, 1)
                        sm["e_dbc"].n += 1

                @blk.gpsimd
                def _(gp):
                    for i in range(64):
                        half, fc = i // 32, i % 32
                        if i >= 4:
                            gp.wait_ge(dv.h, dvF + 2 * (i - 4) + 2)
                        gp.dma_start(
                            out=wst2[i % 4][:, :],
                            in_=d_w2[:, fc * 1024 + half * 512:
                                     fc * 1024 + half * 512 + 512]
                            ).then_inc(sm[w4(i)].h, 16)
                        sm[w4(i)].n += 16

                @blk.scalar
                def _(sc):
                    ao = mk_ao(sc)
                    for st in range(8):
                        if st >= 1:
                            sc.wait_ge(sm["e_do"].h, do0 + st * 8)
                        for fc in range(32):
                            i = st * 32 + fc
                            sc.wait_ge(sm["e_dh"].h, dh0 + i + 1)
                            ph = (p_h[:, 0:256] if fc % 2 == 0 else
                                  p_h[:, 512:768])
                            sc.activation(
                                h_sb[:, fc * 256:(fc + 1) * 256], ph, AF.Gelu,
                                bias=sb_b1[:, fc:fc + 1], scale=1.0
                                ).then_inc(ac.h, 1)
                            ac.n += 1
                        sc.wait_ge(dv.h, drw(st))
                        ao(lambda: sc.activation(dw_d[0:1, :], dw_s[0:1, :],
                                                 AF.Sqrt, bias=EPS, scale=1.0))
                    assert ac.n == acF + 8 * 33

                @blk.vector
                def _(v):
                    vo = mk_vo(v)
                    for t_ in range(64):
                        halft, fct = t_ // 32, t_ % 32
                        v.wait_ge(sm[w4(t_)].h,
                                  w2c[t_ % 4] + 16 * (t_ // 4 + 1))
                        vo(lambda: v.tensor_scalar(
                            m1b[:, :], wst2[t_ % 4][:, :], thrn[:, 1:2],
                            None, op0=ALU.is_le))
                        vo(lambda: v.scalar_tensor_tensor(
                            w2t[:, fct * 1024 + halft * 512:
                                fct * 1024 + halft * 512 + 512],
                            wst2[t_ % 4][:, :], thr[:, 1:2], m1b[:, :],
                            op0=ALU.is_ge, op1=ALU.subtract))
                    for st in range(8):
                        for dc in range(8):
                            i = st * 8 + dc
                            v.wait_ge(sm["e_do"].h, do0 + i + 1)
                            po = (p_o[:, 0:256] if dc % 2 == 0 else
                                  p_o[:, 512:768])
                            vo(lambda: v.scalar_tensor_tensor(
                                res2[:, dc * 256:(dc + 1) * 256], po,
                                sb_b2[:, dc:dc + 1],
                                x1[:, dc * SO + st * 256:
                                   dc * SO + st * 256 + 256],
                                op0=ALU.add, op1=ALU.add))
                            vo(lambda: v.tensor_mul(
                                sq2[dc % 2][:, :],
                                res2[:, dc * 256:(dc + 1) * 256],
                                res2[:, dc * 256:(dc + 1) * 256]))
                        v.wait_ge(sm["e_dsm"].h, dsm0 + st + 1)
                        vo(lambda: v.tensor_scalar(dw_m[0:1, :], p_m2[0:1, :],
                                                   1.0 / 1024.0, None,
                                                   op0=ALU.mult))
                        vo(lambda: v.tensor_scalar(dw_s[0:1, :], p_q2[0:1, :],
                                                   1.0 / 1024.0, None,
                                                   op0=ALU.mult))
                        vo(lambda: v.tensor_mul(dw_t[0:1, :], dw_m[0:1, :],
                                                dw_m[0:1, :]))
                        vo(lambda: v.tensor_sub(dw_s[0:1, :], dw_s[0:1, :],
                                                dw_t[0:1, :]))
                        v.wait_ge(ac.h, dsqr(st))
                        vo(lambda: v.reciprocal(dw_r[0:1, :], dw_d[0:1, :]))
                        v.wait_ge(sm["e_dbc"].h, dbc0 + st + 1)
                        for ec in range(8):
                            i = st * 8 + ec
                            if i >= 2:
                                v.wait_ge(sm[f"s_st{i % 2}"].h,
                                          st0[i % 2] + 16 * (i // 2))
                            vo(lambda: v.tensor_sub(
                                outt[i % 2][:, :],
                                res2[:, ec * 256:(ec + 1) * 256],
                                p_b2[:, 0:256]))
                            vo(lambda: v.tensor_mul(outt[i % 2][:, :],
                                                    outt[i % 2][:, :],
                                                    p_b2[:, 256:512]))
                    assert dv.n == dvF + 128 + 8 * 37

                @blk.sync
                def _(s):
                    for st in range(8):
                        for ec in range(8):
                            i = st * 8 + ec
                            s.wait_ge(dv.h, dout(i))
                            s.dma_start(out=d_out[:, st * 2048 + ec * 256:
                                                  st * 2048 + ec * 256 + 256],
                                        in_=outt[i % 2][:, :]
                                        ).then_inc(sm[f"s_st{i % 2}"].h, 16)
                            sm[f"s_st{i % 2}"].n += 16

        cf.close()

        with nc.Block() as blk:
            @blk.gpsimd
            def _(g):
                g.wait_ge(sm["s_st0"].h, sm["s_st0"].n)
                g.wait_ge(sm["s_st1"].h, sm["s_st1"].n)

    return nc



def _f32c(a):
    return np.ascontiguousarray(a, dtype=np.float32)


def _feat_major(xb):
    T = xb.shape[0]
    t = xb.reshape(T // 512, 512, 8, 128)
    t = t.transpose(3, 0, 2, 1)
    return _f32c(t.reshape(128, T * 8))


def _wsel(w, half):
    r = w.reshape(16, 2, 32, 1024)[:, half]
    r = r.reshape(4, 128, 8, 128)
    r = r.transpose(3, 0, 2, 1)
    return _f32c(r.reshape(128, 4096))


def _wv_prep(wv):
    r = wv.reshape(4, 256, 8, 128).transpose(3, 0, 2, 1)
    return _f32c(r.reshape(128, 8192))


def _wo_prep(wo):
    r = wo.reshape(1024, 8, 128).transpose(2, 1, 0)
    return _f32c(r.reshape(128, 8192))


def _w1_prep(w1):
    r = w1.reshape(4096, 8, 128).transpose(2, 1, 0)
    return _f32c(r.reshape(128, 8 * 4096))


def _w2_prep(w2):
    r = w2.reshape(1024, 32, 128).transpose(2, 1, 0)
    return _f32c(r.reshape(128, 32 * 1024))


def _rope_tables():
    inv = (10000.0 ** (-(np.arange(0, HD, 2, dtype=np.float32) / HD))
           ).astype(np.float32)
    t = np.arange(S, dtype=np.float32)
    fr = t[:, None] * inv[None, :]
    cos = np.cos(fr).astype(np.float32)
    sin = np.sin(fr).astype(np.float32)
    return _f32c(np.tile(cos.T, (4, 1))), _f32c(np.tile(sin.T, (4, 1)))


def make_in_maps(L0_attn_x, L0_attn_w_qkv, L0_attn_w_out, L0_attn_b_out,
                 L0_ffn_W1, L0_ffn_b1, L0_ffn_W2, L0_ffn_b2):
    x = np.asarray(L0_attn_x, dtype=np.float32)
    wqkv = np.asarray(L0_attn_w_qkv, dtype=np.float32)
    wo = np.asarray(L0_attn_w_out, dtype=np.float32)
    bo = np.asarray(L0_attn_b_out, dtype=np.float32)
    W1 = np.asarray(L0_ffn_W1, dtype=np.float32)
    b1 = np.asarray(L0_ffn_b1, dtype=np.float32)
    W2 = np.asarray(L0_ffn_W2, dtype=np.float32)
    b2 = np.asarray(L0_ffn_b2, dtype=np.float32)

    wq, wk, wv = wqkv[0:D], wqkv[D:2 * D], wqkv[2 * D:3 * D]
    ck, sk = _rope_tables()

    shared = {
        "wq1": _wsel(wq, 0), "wq2": _wsel(wq, 1),
        "wk1": _wsel(wk, 0), "wk2": _wsel(wk, 1),
        "wv": _wv_prep(wv), "wo": _wo_prep(wo),
        "w1f": _w1_prep(W1), "w2f": _w2_prep(W2),
        "cos_k": ck, "sin_k": sk,
        "b_out": _f32c(bo.reshape(8, 128).T),
        "b1": _f32c(b1.reshape(32, 128).T),
        "b2": _f32c(b2.reshape(8, 128).T),
        "ident": np.eye(128, dtype=np.float32),
    }

    in_maps = []
    for c in range(8):
        b, half = c // 2, c % 2
        s0 = half * SO
        m = dict(shared)
        m["x_src"] = _feat_major(x[b])
        m["xq_src"] = _feat_major(x[b, s0:s0 + SO])
        m["cos_q"] = _f32c(ck[:, s0:s0 + SO])
        m["sin_q"] = _f32c(sk[:, s0:s0 + SO])
        in_maps.append(m)
    return in_maps


def assemble_out(results):
    out = np.empty((B, S, D), dtype=np.float32)
    for c in range(8):
        b, half = c // 2, c % 2
        o = results[c]["outT"].reshape(128, 8, 8, 256)
        o = o.transpose(1, 3, 2, 0).reshape(SO, D)
        out[b, half * SO:(half + 1) * SO, :] = o
    return out


def kernel(**inputs):
    in_maps = make_in_maps(**inputs)
    nc = build_program()
    res = run_bass_kernel_spmd(nc, in_maps, core_ids=list(range(8)))
    return assemble_out(res.results)


# version 16
# speedup vs baseline: 1.1602x; 1.1602x over previous
import contextlib
import numpy as np

import concourse.bass as bass
import concourse.mybir as mybir
from concourse.bass_utils import run_bass_kernel_spmd

F32 = mybir.dt.float32
BF16 = mybir.dt.bfloat16
AF = mybir.ActivationFunctionType
ALU = mybir.AluOpType
AX = mybir.AxisListType

B, S, D = 4, 4096, 1024
H, HD, HH = 16, 64, 32
SO = 2048
FF = 4096
NG = 4
EXPB = -20.0
SCL = 0.125
EPS = 1e-5
NW = float(FF * D)
SW = 256


class _S:
    __slots__ = ("h", "n")

    def __init__(self, h):
        self.h = h
        self.n = 0


def build_program():
    nc = bass.Bass("TRN2", target_bir_lowering=False)

    def din(name, shape):
        return nc.dram_tensor(name, shape, F32, kind="ExternalInput").ap()

    d_x = din("x_src", [128, 8 * 8 * 512])
    d_xq = din("xq_src", [128, 4 * 8 * 512])
    d_wq1 = din("wq1", [128, NG * 1024])
    d_wq2 = din("wq2", [128, NG * 1024])
    d_wk1 = din("wk1", [128, NG * 1024])
    d_wk2 = din("wk2", [128, NG * 1024])
    d_wv = din("wv", [128, NG * 8 * 256])
    d_wo = din("wo", [128, 8 * 1024])
    d_w1 = din("w1f", [128, 8 * FF])
    d_w2 = din("w2f", [128, 32 * 1024])
    d_ck = din("cos_k", [128, S])
    d_sk = din("sin_k", [128, S])
    d_cq = din("cos_q", [128, SO])
    d_sq = din("sin_q", [128, SO])
    d_bo = din("b_out", [128, 8])
    d_b1 = din("b1", [128, 32])
    d_b2 = din("b2", [128, 8])
    d_id = din("ident", [128, 128])
    d_out = nc.dram_tensor("outT", [128, 8 * 8 * 256], F32,
                           kind="ExternalOutput").ap()
    d_xc = nc.dram_tensor("x_bf_cache", [128, 12 * 4096], BF16).ap()

    ctx = contextlib.ExitStack()
    with ctx:
        consts = ctx.enter_context(nc.sbuf_tensor("consts", [128, 8], F32))
        ones_b = ctx.enter_context(nc.sbuf_tensor("ones_b", [128, 1], BF16))
        ones_f = ctx.enter_context(nc.sbuf_tensor("ones_f", [128, 128], F32))
        sb_bo = ctx.enter_context(nc.sbuf_tensor("sb_bo", [128, 8], F32))
        sb_b1 = ctx.enter_context(nc.sbuf_tensor("sb_b1", [128, 32], F32))
        sb_b2 = ctx.enter_context(nc.sbuf_tensor("sb_b2", [128, 8], F32))
        thr = ctx.enter_context(nc.sbuf_tensor("thr", [128, 2], F32))
        thrn = ctx.enter_context(nc.sbuf_tensor("thrn", [128, 2], F32))
        wsums = ctx.enter_context(nc.sbuf_tensor("wsums", [128, 128], F32))
        trow = ctx.enter_context(nc.sbuf_tensor("trow", [1, 2], F32))
        x1 = ctx.enter_context(nc.sbuf_tensor("x1", [128, 8 * SO], BF16))

        nc.const_aps.register(F32, EXPB, consts[:, 0:1])
        nc.const_aps.register(F32, SCL, consts[:, 1:2])
        nc.const_aps.register(F32, EPS, consts[:, 2:3])
        nc.const_aps.register(F32, 1.0, consts[:, 3:4])

        names = ["s_init", "s_init2", "s_xt0", "s_xt1", "s_xh0", "s_xh1", "s_gw", "s_w10", "s_w11", "s_w20", "s_w21",
                 "s_rk0", "s_rk1", "s_rk2", "s_rk3", "s_xb0", "s_xb1", "s_res0", "s_res1", "s_wo", "s_st0",
                 "s_st1", "e_dv", "e_ac", "e_gp", "e_pj", "e_pv", "e_sc", "e_av",
                 "e_tp", "e_cm", "e_csm", "e_cbc", "e_dh", "e_do",
                 "e_dsm", "e_dbc"]
        sm = {nm: _S(ctx.enter_context(nc.semaphore(nm))) for nm in names}
        dv = sm["e_dv"]
        ac = sm["e_ac"]

        def mk_vo(v):
            def vo(emit):
                v.wait_ge(dv.h, dv.n)
                emit().then_inc(dv.h, 1)
                dv.n += 1
                return dv.n
            return vo

        def mk_ao(sc):
            def ao(emit):
                sc.wait_ge(ac.h, ac.n)
                emit().then_inc(ac.h, 1)
                ac.n += 1
                return ac.n
            return ao

        def mk_go(gp):
            def go(emit):
                gp.wait_ge(sm["e_gp"].h, sm["e_gp"].n)
                emit().then_inc(sm["e_gp"].h, 1)
                sm["e_gp"].n += 1
                return sm["e_gp"].n
            return go

        with nc.Block() as blk:
            @blk.vector
            def _(v):
                vo = mk_vo(v)
                vo(lambda: v.memset(consts[:, 0:1], EXPB))
                vo(lambda: v.memset(consts[:, 1:2], SCL))
                vo(lambda: v.memset(consts[:, 2:3], EPS))
                vo(lambda: v.memset(consts[:, 3:4], 1.0))
                vo(lambda: v.memset(ones_b[:, :], 1.0))
                vo(lambda: v.memset(ones_f[:, :], 1.0))

            @blk.sync
            def _(s):
                for dst, src in ((sb_bo, d_bo), (sb_b1, d_b1), (sb_b2, d_b2)):
                    s.dma_start(out=dst[:, :], in_=src).then_inc(sm["s_init"].h, 16)
                    sm["s_init"].n += 16

        acx = contextlib.ExitStack()
        cf = contextlib.ExitStack()
        if True:
            attn_o = acx.enter_context(
                nc.sbuf_tensor("attn_o", [128, 8 * SO], BF16, side="right"))
            with contextlib.ExitStack() as at:
                def sbr(name, shape, dt=BF16):
                    return at.enter_context(
                        nc.sbuf_tensor(name, shape, dt, side="right"))

                tab_c = sbr("tab_c", [128, S + SO])
                tab_s = sbr("tab_s", [128, S + SO])
                identb = sbr("identb", [128, 128])
                wg = {nm: sbr("w_" + nm, [128, 1024])
                      for nm in ("q1", "q2", "k1", "k2")}
                wvg = sbr("wvg", [128, 2048])
                xt = [sbr(f"xt{i}", [128, 8 * 512]) for i in range(2)]
                rt1 = [sbr(f"rt1_{i}", [128, 512]) for i in range(4)]
                rt2 = [sbr(f"rt2_{i}", [128, 512]) for i in range(4)]
                rscr = sbr("rscr", [128, 512])
                qsA = sbr("qsA", [128, SO])
                qsB = sbr("qsB", [128, SO])
                ksA = sbr("ksA", [128, S])
                ksB = sbr("ksB", [128, S])
                vtok = sbr("vtok", [128, 32 * 4 * 65])
                slabs = [sbr(f"slab{i}", [128, 2 * 32 * 128]) for i in range(2)]
                nrm = [sbr(f"nrm{i}", [128, 64]) for i in range(4)]
                wstr = [at.enter_context(nc.sbuf_tensor(
                    f"wstr{i}", [128, 512], F32, side="right"))
                    for i in range(2)]
                absd = at.enter_context(nc.sbuf_tensor(
                    "absd", [128, 512], BF16, side="right"))
                rcp = at.enter_context(nc.sbuf_tensor("rcp", [128, 2], F32,
                                                      side="right"))

                p_a = at.enter_context(nc.psum_tensor("p_a", [128, 1024], F32))
                p_b = at.enter_context(nc.psum_tensor("p_b", [128, 1024], F32))
                p_m = at.enter_context(nc.psum_tensor("p_m", [128, 1024], F32))
                p_ts = [at.enter_context(nc.psum_tensor(f"p_t{i}", [128, 256],
                                                        BF16)) for i in range(2)]
                PQ = [p_a, p_b]

                with nc.Block() as blk:
                    @blk.gpsimd
                    def _(g):
                        for dst, src, off, w in ((tab_c, d_ck, 0, S),
                                                 (tab_s, d_sk, 0, S),
                                                 (tab_c, d_cq, S, SO),
                                                 (tab_s, d_sq, S, SO)):
                            g.dma_start(out=dst[:, off:off + w], in_=src
                                        ).then_inc(sm["s_init2"].h, 16)
                            sm["s_init2"].n += 16
                        g.dma_start(out=identb[:, :], in_=d_id
                                    ).then_inc(sm["s_init2"].h, 16)
                        sm["s_init2"].n += 16

                initN = sm["s_init"].n
                init2N = sm["s_init2"].n

                with nc.Block() as blk:
                  for g in range(NG):
                    pj0 = sm["e_pj"].n
                    pv0 = sm["e_pv"].n
                    xt0 = [sm["s_xt0"].n, sm["s_xt1"].n]
                    xh0 = [sm["s_xh0"].n, sm["s_xh1"].n]
                    gw0 = sm["s_gw"].n
                    rk0 = [sm[f"s_rk{i}"].n for i in range(4)]
                    dvP = dv.n
                    acP = ac.n
                    w1b = [sm["s_w10"].n, sm["s_w11"].n]
                    def rop(rt, j):
                        if rt < 4:
                            return dvP + 1 + 6 * rt + j
                        return dvP + 25 + (rt - 4) * 10 + j

                    def vev(vi):
                        return dvP + 25 + (vi // 4) * 10 + 6 + (vi % 4) + 1

                    if True:
                        @blk.gpsimd
                        def _(gp):
                            if g > 0:
                                gp.wait_ge(sm["e_pj"].h, pj0)
                                gp.wait_ge(sm["e_pv"].h, pv0)
                            for nm, src in (("q1", d_wq1), ("q2", d_wq2),
                                            ("k1", d_wk1), ("k2", d_wk2)):
                                gp.dma_start(out=wg[nm][:, :],
                                             in_=src[:, g * 1024:(g + 1) * 1024]
                                             ).then_inc(sm["s_gw"].h, 16)
                                sm["s_gw"].n += 16
                            gp.dma_start(out=wvg[:, :],
                                         in_=d_wv[:, g * 2048:(g + 1) * 2048]
                                         ).then_inc(sm["s_gw"].h, 16)
                            sm["s_gw"].n += 16
                            if g == 0:
                                for i in range(12):
                                    if 2 <= i < 6:
                                        gp.wait_ge(sm["e_pj"].h, pj0 + i - 1)
                                    elif i >= 6:
                                        gp.wait_ge(sm["e_pv"].h,
                                                   pv0 + 4 * (i - 2 - 4) + 4)
                                    if i >= 2:
                                        gp.wait_ge(sm[f"s_xb{i % 2}"].h,
                                                   16 * (i // 2))
                                    src = (d_xq[:, i * 4096:(i + 1) * 4096]
                                           if i < 4 else
                                           d_x[:, (i - 4) * 4096:
                                               (i - 3) * 4096])
                                    gp.dma_start(out=xt[i % 2][:, :], in_=src
                                                 ).then_inc(
                                                     sm[f"s_xt{i % 2}"].h, 16)
                                    sm[f"s_xt{i % 2}"].n += 16
                                    gp.wait_ge(sm[f"s_xt{i % 2}"].h,
                                               xt0[i % 2] + 16 * (i // 2 + 1))
                                    gp.dma_start(
                                        out=d_xc[:, i * 4096:(i + 1) * 4096],
                                        in_=xt[i % 2][:, :]
                                        ).then_inc(sm[f"s_xb{i % 2}"].h, 16)
                                    sm[f"s_xb{i % 2}"].n += 16
                            for i in range(32):
                                gi = g * 32 + i
                                if i >= 2:
                                    gp.wait_ge(ac.h, acP + 256 + i - 1)
                                elif g > 0:
                                    gp.wait_ge(ac.h, acP)
                                src_ = (d_w1[:, gi * 512:(gi + 1) * 512]
                                        if gi < 64 else
                                        d_w2[:, (gi - 64) * 512:
                                             (gi - 63) * 512])
                                gp.dma_start(out=wstr[i % 2][:, :], in_=src_
                                             ).then_inc(sm[f"s_w1{i % 2}"].h, 16)
                                sm[f"s_w1{i % 2}"].n += 16

                        @blk.tensor
                        def _(t):
                            t.wait_ge(sm["s_gw"].h, gw0 + 80)
                            for rt in range(12):
                                if g == 0:
                                    t.wait_ge(sm[f"s_xt{rt % 2}"].h,
                                              xt0[rt % 2] + 16 * (rt // 2 + 1))
                                else:
                                    t.wait_ge(sm[f"s_xh{rt % 2}"].h,
                                              xh0[rt % 2] + 16 * (rt // 2 + 1))
                                if rt >= 2:
                                    t.wait_ge(dv.h, rop(rt - 2, 5))
                                elif g > 0:
                                    t.wait_ge(ac.h, acP - 32)
                                w1_, w2_ = (("q1", "q2") if rt < 4 else
                                            ("k1", "k2"))
                                pq = PQ[rt % 2]
                                for dc in range(8):
                                    t.matmul(pq[:, 0:512],
                                             wg[w1_][:, dc * 128:(dc + 1) * 128],
                                             xt[rt % 2][:, dc * 512:(dc + 1) * 512],
                                             start=(dc == 0), stop=(dc == 7))
                                for dc in range(8):
                                    ins = t.matmul(
                                        pq[:, 512:1024],
                                        wg[w2_][:, dc * 128:(dc + 1) * 128],
                                        xt[rt % 2][:, dc * 512:(dc + 1) * 512],
                                        start=(dc == 0), stop=(dc == 7))
                                ins.then_inc(sm["e_pj"].h, 1)
                                sm["e_pj"].n += 1
                                if rt < 4:
                                    continue
                                for sub in range(4):
                                    vi = (rt - 4) * 4 + sub
                                    if vi >= 2:
                                        t.wait_ge(dv.h, vev(vi - 2))
                                    elif g > 0:
                                        t.wait_ge(dv.h, dv_att_end)
                                    pv = (p_m[:, 0:256] if vi % 2 == 0 else
                                          p_m[:, 512:768])
                                    for dc in range(8):
                                        ins = t.matmul(
                                            pv,
                                            xt[rt % 2][:, dc * 512 + sub * 128:
                                                       dc * 512 + sub * 128
                                                       + 128],
                                            wvg[:, dc * 256:(dc + 1) * 256],
                                            start=(dc == 0), stop=(dc == 7))
                                    ins.then_inc(sm["e_pv"].h, 1)
                                    sm["e_pv"].n += 1

                        @blk.vector
                        def _(v):
                            vo = mk_vo(v)
                            if g == 0:
                                v.wait_ge(sm["s_init"].h, initN)
                                v.wait_ge(sm["s_init2"].h, init2N)
                            ones_view = vtok[:, :].rearrange(
                                "p (t h c) -> p t h c", t=32, h=4)[:, :, :, 64:65]
                            vo(lambda: v.memset(ones_view, 1.0))
                            for rt in range(12):
                                v.wait_ge(sm["e_pj"].h, pj0 + rt + 1)
                                if rt >= 4:
                                    v.wait_ge(sm[f"s_rk{rt % 4}"].h,
                                              rk0[rt % 4] + 128 * (rt // 4))
                                elif g > 0:
                                    v.wait_ge(sm[f"s_rk{rt % 4}"].h,
                                              rk0[rt % 4])
                                pq = PQ[rt % 2]
                                toff = ((S + rt * 512) if rt < 4 else
                                        ((rt - 4) * 512))
                                tc = tab_c[:, toff:toff + 512]
                                tsn = tab_s[:, toff:toff + 512]
                                r1, r2 = rt1[rt % 4], rt2[rt % 4]
                                vo(lambda: v.tensor_mul(r1[:, :], pq[:, 0:512],
                                                        tc))
                                vo(lambda: v.tensor_mul(rscr[:, :],
                                                        pq[:, 512:1024], tsn))
                                vo(lambda: v.tensor_sub(r1[:, :], r1[:, :],
                                                        rscr[:, :]))
                                vo(lambda: v.tensor_mul(r2[:, :], pq[:, 0:512],
                                                        tsn))
                                vo(lambda: v.tensor_mul(rscr[:, :],
                                                        pq[:, 512:1024], tc))
                                vo(lambda: v.tensor_add(r2[:, :], r2[:, :],
                                                        rscr[:, :]))
                                if rt >= 4:
                                    for sub in range(4):
                                        vi = (rt - 4) * 4 + sub
                                        v.wait_ge(sm["e_pv"].h, pv0 + vi + 1)
                                        pv = (p_m[:, 0:256] if vi % 2 == 0 else
                                              p_m[:, 512:768])
                                        dst = vtok[:, :].rearrange(
                                            "p (t h c) -> p t h c", t=32, h=4
                                            )[:, vi:vi + 1, :, 0:64]
                                        vo(lambda: v.tensor_copy(dst, pv))
                            assert dv.n == dvP + 1 + 24 + 8 * 10

                        def emit_restack(eng, rts):
                            for rt in rts:
                                eng.wait_ge(dv.h, rop(rt, 6))
                                r1, r2 = rt1[rt % 4], rt2[rt % 4]
                                if rt < 4:
                                    dsts = (qsA, qsB)
                                    off = rt * 512
                                else:
                                    dsts = (ksA, ksB)
                                    off = (rt - 4) * 512
                                for half, dstT in enumerate(dsts):
                                    b0 = half * 64
                                    for hh, srcT in ((0, r1), (32, r2)):
                                        eng.dma_start(
                                            out=dstT[hh:hh + 32, off:off + 512],
                                            in_=srcT[b0:b0 + 32, :]
                                            ).then_inc(sm[f"s_rk{rt % 4}"].h, 16)
                                        sm[f"s_rk{rt % 4}"].n += 16
                                        eng.dma_start(
                                            out=dstT[64 + hh:96 + hh,
                                                     off:off + 512],
                                            in_=srcT[b0 + 32:b0 + 64, :]
                                            ).then_inc(sm[f"s_rk{rt % 4}"].h, 16)
                                        sm[f"s_rk{rt % 4}"].n += 16

                        def emit_xload(eng, i):
                            if 2 <= i < 6:
                                eng.wait_ge(sm["e_pj"].h, pj0 + i - 1)
                            elif i >= 6:
                                eng.wait_ge(sm["e_pv"].h,
                                            pv0 + 4 * (i - 2 - 4) + 4)
                            else:
                                eng.wait_ge(sm["e_pv"].h, pv0)
                            if g == 1:
                                eng.wait_ge(sm[f"s_xb{i % 2}"].h,
                                            sm[f"s_xb{i % 2}"].n)
                            eng.dma_start(
                                out=xt[i % 2][:, :],
                                in_=d_xc[:, i * 4096:(i + 1) * 4096]
                                ).then_inc(sm[f"s_xh{i % 2}"].h, 16)
                            sm[f"s_xh{i % 2}"].n += 16

                        def emit_queue(eng, par):
                            if g > 0:
                                eng.wait_ge(sm["e_av"].h, sm["e_av"].n)
                                for i in range(par, 6, 2):
                                    emit_xload(eng, i)
                                for r in range(par, 6, 2):
                                    emit_restack(eng, (r,))
                                    emit_xload(eng, r + 6)
                                for r in range(par + 6, 12, 2):
                                    emit_restack(eng, (r,))
                            else:
                                emit_restack(eng, range(par, 12, 2))

                        @blk.scalar
                        def _(sc):
                            emit_queue(sc, 1)

                        @blk.sync
                        def _(s):
                            emit_queue(s, 0)

                    sc0 = sm["e_sc"].n
                    av0 = sm["e_av"].n
                    tp0 = sm["e_tp"].n
                    ac0 = ac.n
                    dvA = dv.n
                    nm2 = lambda u, hb: dvA + u * 6 + 2 * (hb + 1)
                    nrm_done = lambda u: dvA + u * 6 + 4
                    ev_done = lambda u: dvA + u * 6 + 6
                    rkN = [sm[f"s_rk{i}"].n for i in range(4)]
                    if True:
                        @blk.tensor
                        def _(t):
                            for i in range(4):
                                t.wait_ge(sm[f"s_rk{i}"].h, rkN[i])
                            for w in range(34):
                                if 1 <= w <= 32 and w - 1 >= 2:
                                    t.wait_ge(dv.h, nrm_done(w - 3))
                                for ttg in range(8):
                                    if w < 32:
                                        u, pr, sg = w, w % 2, w // 2
                                        qs = qsA if pr == 0 else qsB
                                        ks = ksA if pr == 0 else ksB
                                        j = u * 8 + ttg
                                        if j >= 2:
                                            t.wait_ge(ac.h, ac0 + j - 1)
                                        else:
                                            t.wait_ge(dv.h, rop(10 + j, 5))
                                        pq = PQ[j % 2]
                                        for tq in range(4):
                                            tt = ttg * 4 + tq
                                            for hb in range(2):
                                                ins = t.matmul(
                                                    pq[:, hb * 512 + tq * 128:
                                                       hb * 512 + tq * 128 + 128],
                                                    ks[hb * 64:hb * 64 + 64,
                                                       tt * 128:(tt + 1) * 128],
                                                    qs[hb * 64:hb * 64 + 64,
                                                       sg * 128:(sg + 1) * 128],
                                                    start=(tq == 0),
                                                    stop=(tq == 3))
                                        ins.then_inc(sm["e_sc"].h, 1)
                                        sm["e_sc"].n += 1
                                    if 1 <= w <= 32:
                                        u = w - 1
                                        pr = u % 2
                                        if w == 1 and ttg == 0:
                                            t.wait_ge(dv.h, vev(31))
                                        t.wait_ge(ac.h,
                                                  ac0 + u * 8 + ttg + 1)
                                        pav = (p_m[:, 0:130] if u % 2 == 0 else
                                               p_m[:, 512:642])
                                        for hb in range(2):
                                            for tq in range(4):
                                                tt = ttg * 4 + tq
                                                ins = t.matmul(
                                                    pav[:, hb * 65:hb * 65 + 65],
                                                    slabs[pr][:,
                                                              ttg * 1024 +
                                                              hb * 512 +
                                                              tq * 128:
                                                              ttg * 1024 +
                                                              hb * 512 +
                                                              tq * 128 + 128],
                                                    vtok[:, tt * 260 +
                                                         (pr * 2 + hb) * 65:
                                                         tt * 260 +
                                                         (pr * 2 + hb) * 65
                                                         + 65],
                                                    start=(ttg == 0 and
                                                           hb == 0 and tq == 0),
                                                    stop=(ttg == 7 and
                                                          hb == 1 and tq == 3))
                                        ins.then_inc(sm["e_av"].h, 1)
                                        sm["e_av"].n += 1
                                if 2 <= w:
                                    u = w - 2
                                    if u < 32:
                                        pr, sg = u % 2, u // 2
                                        if u >= 4:
                                            t.wait_ge(dv.h, ev_done(u - 2))
                                        for hb in range(2):
                                            hg = 4 * g + pr * 2 + hb
                                            t.wait_ge(dv.h, nm2(u, hb))
                                            bp = 64 * (hg % 2)
                                            ins = t.transpose(
                                                p_ts[u % 2][bp:bp + 64,
                                                            hb * 128:
                                                            hb * 128 + 128],
                                                nrm[(u % 2) * 2 + hb][:, 0:64],
                                                identb[:, :])
                                        ins.then_inc(sm["e_tp"].h, 1)
                                        sm["e_tp"].n += 1

                        @blk.scalar
                        def _(sc):
                            ao = mk_ao(sc)
                            for u in range(32):
                                pr = u % 2
                                if u >= 2:
                                    sc.wait_ge(sm["e_av"].h,
                                               av0 + (u - 2) * 8 + 8)
                                elif g > 0 and u == 0:
                                    sc.wait_ge(sm["e_av"].h, av0)
                                for ttg in range(8):
                                    j = u * 8 + ttg
                                    sc.wait_ge(sm["e_sc"].h, sc0 + j + 1)
                                    pq = PQ[j % 2]
                                    dst = slabs[pr][:, ttg * 1024:
                                                     ttg * 1024 + 1024]
                                    sc.activation(
                                        dst, pq[:, 0:1024], AF.Exp,
                                        bias=EXPB, scale=SCL
                                        ).then_inc(ac.h, 1)
                                    ac.n += 1
                            assert ac.n == ac0 + 256
                            for i in range(32):
                                gi = g * 32 + i
                                sc.wait_ge(sm[f"s_w1{i % 2}"].h,
                                           w1b[i % 2] + 16 * (i // 2 + 1))
                                ao(lambda: sc.activation(
                                    absd[:, :], wstr[i % 2][:, :], AF.Abs,
                                    accum_out=wsums[:, gi:gi + 1]))
                            assert ac.n == ac0 + 288

                        @blk.vector
                        def _(v):
                            vo = mk_vo(v)
                            for u in range(32):
                                pr, sg = u % 2, u // 2
                                v.wait_ge(sm["e_av"].h, av0 + (u + 1) * 8)
                                pav = (p_m[:, 0:130] if u % 2 == 0 else
                                       p_m[:, 512:642])
                                for hb in range(2):
                                    vo(lambda: v.reciprocal(
                                        rcp[:, 0:1],
                                        pav[:, hb * 65 + 64:hb * 65 + 65]))
                                    vo(lambda: v.tensor_scalar(
                                        nrm[(u % 2) * 2 + hb][:, 0:64],
                                        pav[:, hb * 65:hb * 65 + 64],
                                        rcp[:, 0:1], None, op0=ALU.mult))
                                v.wait_ge(sm["e_tp"].h, tp0 + u + 1)
                                for hb in range(2):
                                    hg = 4 * g + pr * 2 + hb
                                    bp = 64 * (hg % 2)
                                    dco = hg // 2
                                    col = dco * SO + sg * 128
                                    vo(lambda: v.tensor_copy(
                                        attn_o[bp:bp + 64, col:col + 128],
                                        p_ts[u % 2][bp:bp + 64,
                                                    hb * 128:hb * 128 + 128]))
                            assert dv.n == dvA + 32 * 6
                        dv_att_end = dv.n

            w1t = cf.enter_context(nc.sbuf_tensor("w1t", [128, 8 * FF], BF16))
            wst2 = [cf.enter_context(nc.sbuf_tensor(f"wst2_{i}", [128, 512],
                                                    F32)) for i in range(4)]
            w4 = lambda i: f"s_w{1 + (i % 4) // 2}{i % 2}"
            m1b = cf.enter_context(nc.sbuf_tensor("m1b", [128, 512], BF16))
            with contextlib.ExitStack() as cx:
                def sbc(name, shape, dt, n=None):
                    if n is None:
                        return cx.enter_context(
                            nc.sbuf_tensor(name, shape, dt, side="right"))
                    return [cx.enter_context(
                        nc.sbuf_tensor(f"{name}{i}", shape, dt, side="right"))
                        for i in range(n)]

                wo_sb = sbc("wo_sb", [128, 8192], BF16)
                res1 = sbc("res1", [128, 4096], F32)
                res1b = sbc("res1b", [128, 4096], BF16)
                sqb = sbc("sqb", [128, 4096], BF16)
                xres = sbc("xres", [128, 512], F32, 2)
                scr = sbc("scr", [128, 512], F32)
                rw_m = sbc("rw_m", [1, 512], F32)
                rw_s = sbc("rw_s", [1, 512], F32)
                rw_t = sbc("rw_t", [1, 512], F32)
                rw_d = sbc("rw_d", [1, 512], F32)
                rw_r = sbc("rw_r", [1, 512], F32)
                p_p = [cx.enter_context(nc.psum_tensor(f"p_p{i}", [128, 512],
                                                       F32)) for i in range(2)]
                p_me = cx.enter_context(nc.psum_tensor("p_me", [1, 512], F32))
                p_sq = cx.enter_context(nc.psum_tensor("p_sq", [1, 512], F32))
                p_bc = cx.enter_context(nc.psum_tensor("p_bc", [128, 1024], F32))

                cm0 = sm["e_cm"].n
                csm0 = sm["e_csm"].n
                cbc0 = sm["e_cbc"].n
                dvC = dv.n
                acC = ac.n
                cev = lambda i: dvC + 6 + (i // 8) * 77 + (i % 8) * 7 + 2
                csq = lambda i: dvC + 6 + (i // 8) * 77 + (i % 8) * 7 + 3
                ctn = lambda t: (dvC + 6 + ((t // 2) // 8) * 77 +
                                 ((t // 2) % 8) * 7 + (5 if t % 2 == 0 else 7))
                crw = lambda st: dvC + 6 + st * 77 + 60
                crc = lambda st: dvC + 6 + st * 77 + 61
                cx1 = lambda st: dvC + 6 + st * 77 + 77
                csd = lambda st: acC + st + 1
                res0 = [sm["s_res0"].n, sm["s_res1"].n]

                w1c = [sm["s_w10"].n, sm["s_w11"].n,
                       sm["s_w20"].n, sm["s_w21"].n]
                with nc.Block() as blk:
                    @blk.gpsimd
                    def _(gp):
                        gp.dma_start(out=wo_sb[:, :], in_=d_wo
                                     ).then_inc(sm["s_wo"].h, 16)
                        sm["s_wo"].n += 16
                        for i in range(64):
                            if i >= 4:
                                gp.wait_ge(dv.h, ctn(i - 4))
                            gp.dma_start(
                                out=wst2[i % 4][:, :],
                                in_=d_w1[:, i * 512:(i + 1) * 512]
                                ).then_inc(sm[w4(i)].h, 16)
                            sm[w4(i)].n += 16

                    @blk.sync
                    def _(s):
                        for i in range(32):
                            st, ec = i // 8, i % 8
                            if i >= 2:
                                s.wait_ge(dv.h, cev(i - 2) - 1)
                            s.dma_start(out=xres[i % 2][:, :],
                                        in_=d_xq[:, st * 4096 + ec * 512:
                                                 st * 4096 + ec * 512 + 512]
                                        ).then_inc(sm[f"s_res{i % 2}"].h, 16)
                            sm[f"s_res{i % 2}"].n += 16

                    @blk.tensor
                    def _(t):
                        t.matmul(p_me[0:1, 0:128], ones_f[:, 0:1],
                                 wsums[:, 0:128],
                                 start=True, stop=True).then_inc(sm["e_csm"].h, 1)
                        sm["e_csm"].n += 1
                        t.wait_ge(dv.h, dvC + 4)
                        t.matmul(p_bc[:, 0:2], ones_f[0:1, 0:128],
                                 trow[0:1, 0:2], start=True,
                                 stop=True).then_inc(sm["e_cbc"].h, 1)
                        sm["e_cbc"].n += 1
                        t.wait_ge(sm["s_wo"].h, sm["s_wo"].n)
                        for st in range(4):
                            if st >= 1:
                                t.wait_ge(dv.h, cx1(st - 1))
                            for ec in range(8):
                                i = st * 8 + ec
                                if i >= 2:
                                    t.wait_ge(dv.h, cev(i - 2))
                                for dc in range(8):
                                    ins = t.matmul(
                                        p_p[i % 2][:, 0:512],
                                        wo_sb[:, dc * 1024 + ec * 128:
                                              dc * 1024 + ec * 128 + 128],
                                        attn_o[:, dc * SO + st * 512:
                                               dc * SO + st * 512 + 512],
                                        start=(dc == 0), stop=(dc == 7))
                                ins.then_inc(sm["e_cm"].h, 1)
                                sm["e_cm"].n += 1
                            for ec in range(8):
                                t.wait_ge(dv.h, cev(st * 8 + ec))
                                t.matmul(p_me[0:1, :], ones_b[:, 0:1],
                                         res1b[:, ec * 512:(ec + 1) * 512],
                                         start=(ec == 0), stop=(ec == 7))
                            for ec in range(8):
                                t.wait_ge(dv.h, csq(st * 8 + ec))
                                ins = t.matmul(p_sq[0:1, :], ones_b[:, 0:1],
                                               sqb[:, ec * 512:(ec + 1) * 512],
                                               start=(ec == 0), stop=(ec == 7))
                            ins.then_inc(sm["e_csm"].h, 1)
                            sm["e_csm"].n += 1
                            t.wait_ge(dv.h, crc(st))
                            t.matmul(p_bc[:, 0:512], ones_f[0:1, 0:128],
                                     rw_m[0:1, :], start=True, stop=True)
                            t.matmul(p_bc[:, 512:1024], ones_f[0:1, 0:128],
                                     rw_r[0:1, :], start=True,
                                     stop=True).then_inc(sm["e_cbc"].h, 1)
                            sm["e_cbc"].n += 1

                    @blk.vector
                    def _(v):
                        vo = mk_vo(v)
                        v.wait_ge(sm["e_csm"].h, csm0 + 1)
                        vo(lambda: v.tensor_reduce(trow[0:1, 0:1],
                                                   p_me[0:1, 0:64], AX.X,
                                                   ALU.add))
                        vo(lambda: v.tensor_reduce(trow[0:1, 1:2],
                                                   p_me[0:1, 64:128], AX.X,
                                                   ALU.add))
                        vo(lambda: v.tensor_scalar(trow[0:1, 0:2],
                                                   trow[0:1, 0:2], 0.5 / NW,
                                                   None, op0=ALU.mult))
                        vo(lambda: v.tensor_scalar(trow[0:1, 0:2],
                                                   trow[0:1, 0:2], 0.5e-8,
                                                   None, op0=ALU.max))
                        v.wait_ge(sm["e_cbc"].h, cbc0 + 1)
                        vo(lambda: v.tensor_copy(thr[:, :], p_bc[:, 0:2]))
                        vo(lambda: v.tensor_scalar(thrn[:, :], p_bc[:, 0:2],
                                                   -1.0, None, op0=ALU.mult))
                        for st in range(4):
                            for ec in range(8):
                                i = st * 8 + ec
                                v.wait_ge(sm["e_cm"].h, cm0 + i + 1)
                                v.wait_ge(sm[f"s_res{i % 2}"].h,
                                          res0[i % 2] + 16 * (i // 2 + 1))
                                vo(lambda: v.scalar_tensor_tensor(
                                    res1[:, ec * 512:(ec + 1) * 512],
                                    p_p[i % 2][:, 0:512], sb_bo[:, ec:ec + 1],
                                    xres[i % 2][:, :],
                                    op0=ALU.add, op1=ALU.add))
                                vo(lambda: v.tensor_copy(
                                    res1b[:, ec * 512:(ec + 1) * 512],
                                    res1[:, ec * 512:(ec + 1) * 512]))
                                vo(lambda: v.tensor_mul(
                                    sqb[:, ec * 512:(ec + 1) * 512],
                                    res1b[:, ec * 512:(ec + 1) * 512],
                                    res1b[:, ec * 512:(ec + 1) * 512]))
                                for tt_ in (2 * i, 2 * i + 1):
                                    v.wait_ge(sm[w4(tt_)].h,
                                              w1c[tt_ % 4] +
                                              16 * (tt_ // 4 + 1))
                                    vo(lambda: v.tensor_scalar(
                                        m1b[:, :], wst2[tt_ % 4][:, :],
                                        thrn[:, 0:1], None, op0=ALU.is_le))
                                    vo(lambda: v.scalar_tensor_tensor(
                                        w1t[:, tt_ * 512:(tt_ + 1) * 512],
                                        wst2[tt_ % 4][:, :], thr[:, 0:1],
                                        m1b[:, :], op0=ALU.is_ge,
                                        op1=ALU.subtract))
                            v.wait_ge(sm["e_csm"].h, csm0 + st + 2)
                            vo(lambda: v.tensor_scalar(rw_m[0:1, :],
                                                       p_me[0:1, :],
                                                       1.0 / 1024.0, None,
                                                       op0=ALU.mult))
                            vo(lambda: v.tensor_scalar(rw_s[0:1, :],
                                                       p_sq[0:1, :],
                                                       1.0 / 1024.0, None,
                                                       op0=ALU.mult))
                            vo(lambda: v.tensor_mul(rw_t[0:1, :], rw_m[0:1, :],
                                                    rw_m[0:1, :]))
                            vo(lambda: v.tensor_sub(rw_s[0:1, :], rw_s[0:1, :],
                                                    rw_t[0:1, :]))
                            v.wait_ge(ac.h, csd(st))
                            vo(lambda: v.reciprocal(rw_r[0:1, :], rw_d[0:1, :]))
                            v.wait_ge(sm["e_cbc"].h, cbc0 + st + 2)
                            for ec in range(8):
                                vo(lambda: v.tensor_sub(
                                    scr[:, :],
                                    res1[:, ec * 512:(ec + 1) * 512],
                                    p_bc[:, 0:512]))
                                vo(lambda: v.tensor_mul(
                                    x1[:, ec * SO + st * 512:
                                       ec * SO + st * 512 + 512],
                                    scr[:, :], p_bc[:, 512:1024]))
                        assert dv.n == dvC + 6 + 4 * 77

                    @blk.scalar
                    def _(sc):
                        ao = mk_ao(sc)
                        for st in range(4):
                            sc.wait_ge(dv.h, crw(st))
                            ao(lambda: sc.activation(rw_d[0:1, :], rw_s[0:1, :],
                                                     AF.Sqrt, bias=EPS,
                                                     scale=1.0))
                        assert ac.n == acC + 4

        acx.close()

        with contextlib.ExitStack() as ds:
            def sbd(name, shape, dt, n=None):
                if n is None:
                    return ds.enter_context(
                        nc.sbuf_tensor(name, shape, dt, side="right"))
                return [ds.enter_context(
                    nc.sbuf_tensor(f"{name}{i}", shape, dt, side="right"))
                    for i in range(n)]

            w2t = sbd("w2t", [128, 32 * 1024], BF16)
            h_sb = sbd("h_sb", [128, 32 * 256], BF16)
            res2 = sbd("res2", [128, 8 * 256], F32)
            sq2 = sbd("sq2_", [128, 256], BF16, 2)
            outt = sbd("outt", [128, 256], F32, 2)
            dw_m = sbd("dw_m", [1, 256], F32)
            dw_s = sbd("dw_s", [1, 256], F32)
            dw_t = sbd("dw_t", [1, 256], F32)
            dw_d = sbd("dw_d", [1, 256], F32)
            dw_r = sbd("dw_r", [1, 256], F32)
            p_h = ds.enter_context(nc.psum_tensor("p_h", [128, 1024], F32))
            p_o = ds.enter_context(nc.psum_tensor("p_o", [128, 1024], F32))
            p_m2 = ds.enter_context(nc.psum_tensor("p_m2", [1, 256], F32))
            p_q2 = ds.enter_context(nc.psum_tensor("p_q2", [1, 256], F32))
            p_b2 = ds.enter_context(nc.psum_tensor("p_b2", [128, 512], F32))

            dh0 = sm["e_dh"].n
            do0 = sm["e_do"].n
            dsm0 = sm["e_dsm"].n
            dbc0 = sm["e_dbc"].n
            dvF = dv.n
            acF = ac.n
            w2c = [sm["s_w10"].n, sm["s_w11"].n,
                   sm["s_w20"].n, sm["s_w21"].n]
            gpT = sm["e_gp"].n
            w20 = [sm["s_w20"].n, sm["s_w21"].n]
            dev = lambda i: dvF + 128 + (i // 8) * 37 + (i % 8) * 2 + 1
            dsq = lambda i: dvF + 128 + (i // 8) * 37 + (i % 8) * 2 + 2
            drw = lambda st: dvF + 128 + st * 37 + 20
            drc = lambda st: dvF + 128 + st * 37 + 21
            dout = lambda i: dvF + 128 + (i // 8) * 37 + 21 + 2 * ((i % 8) + 1)
            dall = lambda st: dvF + 128 + st * 37 + 37
            dgel = lambda i: acF + (i // 32) * 33 + (i % 32) + 1
            dsqr = lambda st: acF + st * 33 + 33
            st0 = [sm["s_st0"].n, sm["s_st1"].n]
            with nc.Block() as blk:
                @blk.tensor
                def _(t):
                    for st in range(8):
                        for fc in range(32):
                            i = st * 32 + fc
                            if i >= 2:
                                t.wait_ge(ac.h, dgel(i - 2))
                            ph = (p_h[:, 0:256] if fc % 2 == 0 else
                                  p_h[:, 512:768])
                            for dc in range(8):
                                ins = t.matmul(
                                    ph,
                                    w1t[:, dc * FF + fc * 128:
                                        dc * FF + fc * 128 + 128],
                                    x1[:, dc * SO + st * 256:
                                       dc * SO + st * 256 + 256],
                                    start=(dc == 0), stop=(dc == 7))
                            ins.then_inc(sm["e_dh"].h, 1)
                            sm["e_dh"].n += 1
                        t.wait_ge(ac.h, dgel(st * 32 + 31))

                        def stats(dc):
                            if st >= 1 and dc == 0:
                                t.wait_ge(dv.h, drw(st - 1))
                            t.wait_ge(dv.h, dev(st * 8 + dc))
                            t.matmul(p_m2[0:1, :], ones_f[:, 0:1],
                                     res2[:, dc * 256:(dc + 1) * 256],
                                     start=(dc == 0), stop=(dc == 7))
                            t.wait_ge(dv.h, dsq(st * 8 + dc))
                            ins = t.matmul(p_q2[0:1, :], ones_b[:, 0:1],
                                           sq2[dc % 2][:, :],
                                           start=(dc == 0), stop=(dc == 7))
                            if dc == 7:
                                ins.then_inc(sm["e_dsm"].h, 1)
                                sm["e_dsm"].n += 1

                        for dc in range(8):
                            i = st * 8 + dc
                            if st == 0:
                                t.wait_ge(dv.h, dvF + (dc // 4 + 1) * 64)
                            if i >= 2:
                                t.wait_ge(dv.h, dev(i - 2))
                            po = (p_o[:, 0:256] if dc % 2 == 0 else
                                  p_o[:, 512:768])
                            for fc in range(32):
                                ins = t.matmul(
                                    po,
                                    w2t[:, fc * 1024 + dc * 128:
                                        fc * 1024 + dc * 128 + 128],
                                    h_sb[:, fc * 256:(fc + 1) * 256],
                                    start=(fc == 0), stop=(fc == 31))
                            ins.then_inc(sm["e_do"].h, 1)
                            sm["e_do"].n += 1
                            if dc >= 1:
                                stats(dc - 1)
                        stats(7)
                        t.wait_ge(dv.h, drc(st))
                        if st >= 1:
                            t.wait_ge(dv.h, dall(st - 1))
                        t.matmul(p_b2[:, 0:256], ones_f[0:1, 0:128],
                                 dw_m[0:1, :], start=True, stop=False)
                        t.matmul(p_b2[:, 256:512], ones_f[0:1, 0:128],
                                 dw_r[0:1, :], start=False, stop=True
                                 ).then_inc(sm["e_dbc"].h, 1)
                        sm["e_dbc"].n += 1

                @blk.gpsimd
                def _(gp):
                    for i in range(64):
                        half, fc = i // 32, i % 32
                        if i >= 4:
                            gp.wait_ge(dv.h, dvF + 2 * (i - 4) + 2)
                        gp.dma_start(
                            out=wst2[i % 4][:, :],
                            in_=d_w2[:, fc * 1024 + half * 512:
                                     fc * 1024 + half * 512 + 512]
                            ).then_inc(sm[w4(i)].h, 16)
                        sm[w4(i)].n += 16

                @blk.scalar
                def _(sc):
                    ao = mk_ao(sc)
                    for st in range(8):
                        if st >= 1:
                            sc.wait_ge(sm["e_do"].h, do0 + st * 8)
                        for fc in range(32):
                            i = st * 32 + fc
                            sc.wait_ge(sm["e_dh"].h, dh0 + i + 1)
                            ph = (p_h[:, 0:256] if fc % 2 == 0 else
                                  p_h[:, 512:768])
                            sc.activation(
                                h_sb[:, fc * 256:(fc + 1) * 256], ph, AF.Gelu,
                                bias=sb_b1[:, fc:fc + 1], scale=1.0
                                ).then_inc(ac.h, 1)
                            ac.n += 1
                        sc.wait_ge(dv.h, drw(st))
                        ao(lambda: sc.activation(dw_d[0:1, :], dw_s[0:1, :],
                                                 AF.Sqrt, bias=EPS, scale=1.0))
                    assert ac.n == acF + 8 * 33

                @blk.vector
                def _(v):
                    vo = mk_vo(v)
                    for t_ in range(64):
                        halft, fct = t_ // 32, t_ % 32
                        v.wait_ge(sm[w4(t_)].h,
                                  w2c[t_ % 4] + 16 * (t_ // 4 + 1))
                        vo(lambda: v.tensor_scalar(
                            m1b[:, :], wst2[t_ % 4][:, :], thrn[:, 1:2],
                            None, op0=ALU.is_le))
                        vo(lambda: v.scalar_tensor_tensor(
                            w2t[:, fct * 1024 + halft * 512:
                                fct * 1024 + halft * 512 + 512],
                            wst2[t_ % 4][:, :], thr[:, 1:2], m1b[:, :],
                            op0=ALU.is_ge, op1=ALU.subtract))
                    for st in range(8):
                        for dc in range(8):
                            i = st * 8 + dc
                            v.wait_ge(sm["e_do"].h, do0 + i + 1)
                            po = (p_o[:, 0:256] if dc % 2 == 0 else
                                  p_o[:, 512:768])
                            vo(lambda: v.scalar_tensor_tensor(
                                res2[:, dc * 256:(dc + 1) * 256], po,
                                sb_b2[:, dc:dc + 1],
                                x1[:, dc * SO + st * 256:
                                   dc * SO + st * 256 + 256],
                                op0=ALU.add, op1=ALU.add))
                            vo(lambda: v.tensor_mul(
                                sq2[dc % 2][:, :],
                                res2[:, dc * 256:(dc + 1) * 256],
                                res2[:, dc * 256:(dc + 1) * 256]))
                        v.wait_ge(sm["e_dsm"].h, dsm0 + st + 1)
                        vo(lambda: v.tensor_scalar(dw_m[0:1, :], p_m2[0:1, :],
                                                   1.0 / 1024.0, None,
                                                   op0=ALU.mult))
                        vo(lambda: v.tensor_scalar(dw_s[0:1, :], p_q2[0:1, :],
                                                   1.0 / 1024.0, None,
                                                   op0=ALU.mult))
                        vo(lambda: v.tensor_mul(dw_t[0:1, :], dw_m[0:1, :],
                                                dw_m[0:1, :]))
                        vo(lambda: v.tensor_sub(dw_s[0:1, :], dw_s[0:1, :],
                                                dw_t[0:1, :]))
                        v.wait_ge(ac.h, dsqr(st))
                        vo(lambda: v.reciprocal(dw_r[0:1, :], dw_d[0:1, :]))
                        v.wait_ge(sm["e_dbc"].h, dbc0 + st + 1)
                        for ec in range(8):
                            i = st * 8 + ec
                            if i >= 2:
                                v.wait_ge(sm[f"s_st{i % 2}"].h,
                                          st0[i % 2] + 16 * (i // 2))
                            vo(lambda: v.tensor_sub(
                                outt[i % 2][:, :],
                                res2[:, ec * 256:(ec + 1) * 256],
                                p_b2[:, 0:256]))
                            vo(lambda: v.tensor_mul(outt[i % 2][:, :],
                                                    outt[i % 2][:, :],
                                                    p_b2[:, 256:512]))
                    assert dv.n == dvF + 128 + 8 * 37

                @blk.sync
                def _(s):
                    for st in range(8):
                        for ec in range(8):
                            i = st * 8 + ec
                            s.wait_ge(dv.h, dout(i))
                            s.dma_start(out=d_out[:, st * 2048 + ec * 256:
                                                  st * 2048 + ec * 256 + 256],
                                        in_=outt[i % 2][:, :]
                                        ).then_inc(sm[f"s_st{i % 2}"].h, 16)
                            sm[f"s_st{i % 2}"].n += 16

        cf.close()

        with nc.Block() as blk:
            @blk.gpsimd
            def _(g):
                g.wait_ge(sm["s_st0"].h, sm["s_st0"].n)
                g.wait_ge(sm["s_st1"].h, sm["s_st1"].n)

    return nc



def _f32c(a):
    return np.ascontiguousarray(a, dtype=np.float32)


def _feat_major(xb):
    T = xb.shape[0]
    t = xb.reshape(T // 512, 512, 8, 128)
    t = t.transpose(3, 0, 2, 1)
    return _f32c(t.reshape(128, T * 8))


def _wsel(w, half):
    r = w.reshape(16, 2, 32, 1024)[:, half]
    r = r.reshape(4, 128, 8, 128)
    r = r.transpose(3, 0, 2, 1)
    return _f32c(r.reshape(128, 4096))


def _wv_prep(wv):
    r = wv.reshape(4, 256, 8, 128).transpose(3, 0, 2, 1)
    return _f32c(r.reshape(128, 8192))


def _wo_prep(wo):
    r = wo.reshape(1024, 8, 128).transpose(2, 1, 0)
    return _f32c(r.reshape(128, 8192))


def _w1_prep(w1):
    r = w1.reshape(4096, 8, 128).transpose(2, 1, 0)
    return _f32c(r.reshape(128, 8 * 4096))


def _w2_prep(w2):
    r = w2.reshape(1024, 32, 128).transpose(2, 1, 0)
    return _f32c(r.reshape(128, 32 * 1024))


def _rope_tables():
    inv = (10000.0 ** (-(np.arange(0, HD, 2, dtype=np.float32) / HD))
           ).astype(np.float32)
    t = np.arange(S, dtype=np.float32)
    fr = t[:, None] * inv[None, :]
    cos = np.cos(fr).astype(np.float32)
    sin = np.sin(fr).astype(np.float32)
    return _f32c(np.tile(cos.T, (4, 1))), _f32c(np.tile(sin.T, (4, 1)))


def make_in_maps(L0_attn_x, L0_attn_w_qkv, L0_attn_w_out, L0_attn_b_out,
                 L0_ffn_W1, L0_ffn_b1, L0_ffn_W2, L0_ffn_b2):
    x = np.asarray(L0_attn_x, dtype=np.float32)
    wqkv = np.asarray(L0_attn_w_qkv, dtype=np.float32)
    wo = np.asarray(L0_attn_w_out, dtype=np.float32)
    bo = np.asarray(L0_attn_b_out, dtype=np.float32)
    W1 = np.asarray(L0_ffn_W1, dtype=np.float32)
    b1 = np.asarray(L0_ffn_b1, dtype=np.float32)
    W2 = np.asarray(L0_ffn_W2, dtype=np.float32)
    b2 = np.asarray(L0_ffn_b2, dtype=np.float32)

    wq, wk, wv = wqkv[0:D], wqkv[D:2 * D], wqkv[2 * D:3 * D]
    ck, sk = _rope_tables()

    shared = {
        "wq1": _wsel(wq, 0), "wq2": _wsel(wq, 1),
        "wk1": _wsel(wk, 0), "wk2": _wsel(wk, 1),
        "wv": _wv_prep(wv), "wo": _wo_prep(wo),
        "w1f": _w1_prep(W1), "w2f": _w2_prep(W2),
        "cos_k": ck, "sin_k": sk,
        "b_out": _f32c(bo.reshape(8, 128).T),
        "b1": _f32c(b1.reshape(32, 128).T),
        "b2": _f32c(b2.reshape(8, 128).T),
        "ident": np.eye(128, dtype=np.float32),
    }

    in_maps = []
    for c in range(8):
        b, half = c // 2, c % 2
        s0 = half * SO
        m = dict(shared)
        m["x_src"] = _feat_major(x[b])
        m["xq_src"] = _feat_major(x[b, s0:s0 + SO])
        m["cos_q"] = _f32c(ck[:, s0:s0 + SO])
        m["sin_q"] = _f32c(sk[:, s0:s0 + SO])
        in_maps.append(m)
    return in_maps


def assemble_out(results):
    out = np.empty((B, S, D), dtype=np.float32)
    for c in range(8):
        b, half = c // 2, c % 2
        o = results[c]["outT"].reshape(128, 8, 8, 256)
        o = o.transpose(1, 3, 2, 0).reshape(SO, D)
        out[b, half * SO:(half + 1) * SO, :] = o
    return out


def kernel(**inputs):
    in_maps = make_in_maps(**inputs)
    nc = build_program()
    res = run_bass_kernel_spmd(nc, in_maps, core_ids=list(range(8)))
    return assemble_out(res.results)


# version 17 (final)
# speedup vs baseline: 1.1855x; 1.0218x over previous
import contextlib
import numpy as np

import concourse.bass as bass
import concourse.mybir as mybir
from concourse.bass_utils import run_bass_kernel_spmd

F32 = mybir.dt.float32
BF16 = mybir.dt.bfloat16
AF = mybir.ActivationFunctionType
ALU = mybir.AluOpType
AX = mybir.AxisListType

B, S, D = 4, 4096, 1024
H, HD, HH = 16, 64, 32
SO = 2048
FF = 4096
NG = 4
EXPB = -20.0
SCL = 0.125
EPS = 1e-5
NW = float(FF * D)
SW = 256


class _S:
    __slots__ = ("h", "n")

    def __init__(self, h):
        self.h = h
        self.n = 0


def build_program():
    nc = bass.Bass("TRN2", target_bir_lowering=False)

    def din(name, shape):
        return nc.dram_tensor(name, shape, F32, kind="ExternalInput").ap()

    d_x = din("x_src", [128, 8 * 8 * 512])
    d_xq = din("xq_src", [128, 4 * 8 * 512])
    d_wq1 = din("wq1", [128, NG * 1024])
    d_wq2 = din("wq2", [128, NG * 1024])
    d_wk1 = din("wk1", [128, NG * 1024])
    d_wk2 = din("wk2", [128, NG * 1024])
    d_wv = din("wv", [128, NG * 8 * 256])
    d_wo = din("wo", [128, 8 * 1024])
    d_w1 = din("w1f", [128, 8 * FF])
    d_w2 = din("w2f", [128, 32 * 1024])
    d_ck = din("cos_k", [128, S])
    d_sk = din("sin_k", [128, S])
    d_cq = din("cos_q", [128, SO])
    d_sq = din("sin_q", [128, SO])
    d_bo = din("b_out", [128, 8])
    d_b1 = din("b1", [128, 32])
    d_b2 = din("b2", [128, 8])
    d_id = din("ident", [128, 128])
    d_out = nc.dram_tensor("outT", [128, 8 * 8 * 256], F32,
                           kind="ExternalOutput").ap()
    d_xc = nc.dram_tensor("x_bf_cache", [128, 12 * 4096], BF16).ap()

    ctx = contextlib.ExitStack()
    with ctx:
        consts = ctx.enter_context(nc.sbuf_tensor("consts", [128, 8], F32))
        ones_b = ctx.enter_context(nc.sbuf_tensor("ones_b", [128, 1], BF16))
        ones_f = ctx.enter_context(nc.sbuf_tensor("ones_f", [128, 128], F32))
        sb_bo = ctx.enter_context(nc.sbuf_tensor("sb_bo", [128, 8], F32))
        sb_b1 = ctx.enter_context(nc.sbuf_tensor("sb_b1", [128, 32], F32))
        sb_b2 = ctx.enter_context(nc.sbuf_tensor("sb_b2", [128, 8], F32))
        thr = ctx.enter_context(nc.sbuf_tensor("thr", [128, 2], F32))
        thrn = ctx.enter_context(nc.sbuf_tensor("thrn", [128, 2], F32))
        wsums = ctx.enter_context(nc.sbuf_tensor("wsums", [128, 128], F32))
        trow = ctx.enter_context(nc.sbuf_tensor("trow", [1, 2], F32))
        x1 = ctx.enter_context(nc.sbuf_tensor("x1", [128, 8 * SO], BF16))

        nc.const_aps.register(F32, EXPB, consts[:, 0:1])
        nc.const_aps.register(F32, SCL, consts[:, 1:2])
        nc.const_aps.register(F32, EPS, consts[:, 2:3])
        nc.const_aps.register(F32, 1.0, consts[:, 3:4])

        names = ["s_init", "s_init2", "s_xt0", "s_xt1", "s_xh0", "s_xh1", "s_gw", "s_w10", "s_w11", "s_w20", "s_w21",
                 "s_rk0", "s_rk1", "s_rk2", "s_rk3", "s_xb0", "s_xb1", "s_res0", "s_res1", "s_wo", "s_st0",
                 "s_st1", "e_dv", "e_ac", "e_gp", "e_pj", "e_pv", "e_sc", "e_av",
                 "e_tp", "e_cm", "e_csm", "e_cbc", "e_dh", "e_do",
                 "e_dsm", "e_dbc"]
        sm = {nm: _S(ctx.enter_context(nc.semaphore(nm))) for nm in names}
        dv = sm["e_dv"]
        ac = sm["e_ac"]

        def mk_vo(v):
            def vo(emit):
                v.wait_ge(dv.h, dv.n)
                emit().then_inc(dv.h, 1)
                dv.n += 1
                return dv.n
            return vo

        def mk_ao(sc):
            def ao(emit):
                sc.wait_ge(ac.h, ac.n)
                emit().then_inc(ac.h, 1)
                ac.n += 1
                return ac.n
            return ao

        def mk_go(gp):
            def go(emit):
                gp.wait_ge(sm["e_gp"].h, sm["e_gp"].n)
                emit().then_inc(sm["e_gp"].h, 1)
                sm["e_gp"].n += 1
                return sm["e_gp"].n
            return go

        with nc.Block() as blk:
            @blk.vector
            def _(v):
                vo = mk_vo(v)
                vo(lambda: v.memset(consts[:, 0:1], EXPB))
                vo(lambda: v.memset(consts[:, 1:2], SCL))
                vo(lambda: v.memset(consts[:, 2:3], EPS))
                vo(lambda: v.memset(consts[:, 3:4], 1.0))
                vo(lambda: v.memset(ones_b[:, :], 1.0))
                vo(lambda: v.memset(ones_f[:, :], 1.0))

            @blk.sync
            def _(s):
                for dst, src in ((sb_bo, d_bo), (sb_b1, d_b1), (sb_b2, d_b2)):
                    s.dma_start(out=dst[:, :], in_=src).then_inc(sm["s_init"].h, 16)
                    sm["s_init"].n += 16

        acx = contextlib.ExitStack()
        cf = contextlib.ExitStack()
        if True:
            attn_o = acx.enter_context(
                nc.sbuf_tensor("attn_o", [128, 8 * SO], BF16, side="right"))
            with contextlib.ExitStack() as at:
                def sbr(name, shape, dt=BF16):
                    return at.enter_context(
                        nc.sbuf_tensor(name, shape, dt, side="right"))

                tab_c = sbr("tab_c", [128, S + SO])
                tab_s = sbr("tab_s", [128, S + SO])
                identb = sbr("identb", [128, 128])
                wg = {nm: sbr("w_" + nm, [128, 1024])
                      for nm in ("q1", "q2", "k1", "k2")}
                wvg = sbr("wvg", [128, 2048])
                xt = [sbr(f"xt{i}", [128, 8 * 512]) for i in range(2)]
                rt1 = [sbr(f"rt1_{i}", [128, 512]) for i in range(4)]
                rt2 = [sbr(f"rt2_{i}", [128, 512]) for i in range(4)]
                rscr = sbr("rscr", [128, 512])
                qsA = sbr("qsA", [128, SO])
                qsB = sbr("qsB", [128, SO])
                ksA = sbr("ksA", [128, S])
                ksB = sbr("ksB", [128, S])
                vtok = sbr("vtok", [128, 32 * 4 * 65])
                slabs = [sbr(f"slab{i}", [128, 2 * 32 * 128]) for i in range(2)]
                nrm = [sbr(f"nrm{i}", [128, 64]) for i in range(4)]
                wstr = [at.enter_context(nc.sbuf_tensor(
                    f"wstr{i}", [128, 512], F32, side="right"))
                    for i in range(2)]
                absd = at.enter_context(nc.sbuf_tensor(
                    "absd", [128, 512], BF16, side="right"))
                rcp = at.enter_context(nc.sbuf_tensor("rcp", [128, 2], F32,
                                                      side="right"))

                p_a = at.enter_context(nc.psum_tensor("p_a", [128, 1024], F32))
                p_b = at.enter_context(nc.psum_tensor("p_b", [128, 1024], F32))
                p_m = at.enter_context(nc.psum_tensor("p_m", [128, 1024], F32))
                p_ts = [at.enter_context(nc.psum_tensor(f"p_t{i}", [128, 256],
                                                        BF16)) for i in range(2)]
                PQ = [p_a, p_b]

                with nc.Block() as blk:
                    @blk.gpsimd
                    def _(g):
                        for dst, src, off, w in ((tab_c, d_ck, 0, S),
                                                 (tab_s, d_sk, 0, S),
                                                 (tab_c, d_cq, S, SO),
                                                 (tab_s, d_sq, S, SO)):
                            g.dma_start(out=dst[:, off:off + w], in_=src
                                        ).then_inc(sm["s_init2"].h, 16)
                            sm["s_init2"].n += 16
                        g.dma_start(out=identb[:, :], in_=d_id
                                    ).then_inc(sm["s_init2"].h, 16)
                        sm["s_init2"].n += 16

                initN = sm["s_init"].n
                init2N = sm["s_init2"].n

                with nc.Block() as blk:
                  for g in range(NG):
                    pj0 = sm["e_pj"].n
                    pv0 = sm["e_pv"].n
                    xt0 = [sm["s_xt0"].n, sm["s_xt1"].n]
                    xh0 = [sm["s_xh0"].n, sm["s_xh1"].n]
                    gw0 = sm["s_gw"].n
                    rk0 = [sm[f"s_rk{i}"].n for i in range(4)]
                    dvP = dv.n
                    acP = ac.n
                    w1b = [sm["s_w10"].n, sm["s_w11"].n]
                    def rop(rt, j):
                        if rt < 4:
                            return dvP + 1 + 6 * rt + j
                        return dvP + 25 + (rt - 4) * 10 + j

                    def vev(vi):
                        return dvP + 25 + (vi // 4) * 10 + 6 + (vi % 4) + 1

                    if True:
                        @blk.gpsimd
                        def _(gp):
                            if g > 0:
                                gp.wait_ge(sm["e_pj"].h, pj0)
                                gp.wait_ge(sm["e_pv"].h, pv0)
                            for nm, src in (("q1", d_wq1), ("q2", d_wq2),
                                            ("k1", d_wk1), ("k2", d_wk2)):
                                gp.dma_start(out=wg[nm][:, :],
                                             in_=src[:, g * 1024:(g + 1) * 1024]
                                             ).then_inc(sm["s_gw"].h, 16)
                                sm["s_gw"].n += 16
                            gp.dma_start(out=wvg[:, :],
                                         in_=d_wv[:, g * 2048:(g + 1) * 2048]
                                         ).then_inc(sm["s_gw"].h, 16)
                            sm["s_gw"].n += 16
                            if g == 0:
                                for i in range(12):
                                    if 2 <= i < 6:
                                        gp.wait_ge(sm["e_pj"].h, pj0 + i - 1)
                                    elif i >= 6:
                                        gp.wait_ge(sm["e_pv"].h,
                                                   pv0 + 4 * (i - 2 - 4) + 4)
                                    if i >= 2:
                                        gp.wait_ge(sm[f"s_xb{i % 2}"].h,
                                                   16 * (i // 2))
                                    src = (d_xq[:, i * 4096:(i + 1) * 4096]
                                           if i < 4 else
                                           d_x[:, (i - 4) * 4096:
                                               (i - 3) * 4096])
                                    gp.dma_start(out=xt[i % 2][:, :], in_=src
                                                 ).then_inc(
                                                     sm[f"s_xt{i % 2}"].h, 16)
                                    sm[f"s_xt{i % 2}"].n += 16
                                    gp.wait_ge(sm[f"s_xt{i % 2}"].h,
                                               xt0[i % 2] + 16 * (i // 2 + 1))
                                    gp.dma_start(
                                        out=d_xc[:, i * 4096:(i + 1) * 4096],
                                        in_=xt[i % 2][:, :]
                                        ).then_inc(sm[f"s_xb{i % 2}"].h, 16)
                                    sm[f"s_xb{i % 2}"].n += 16
                            else:
                                for i in range(12):
                                    if 2 <= i < 6:
                                        gp.wait_ge(sm["e_pj"].h, pj0 + i - 1)
                                    elif i >= 6:
                                        gp.wait_ge(sm["e_pv"].h,
                                                   pv0 + 4 * (i - 2 - 4) + 4)
                                    if g == 1 and i < 2:
                                        gp.wait_ge(sm[f"s_xb{i % 2}"].h,
                                                   sm[f"s_xb{i % 2}"].n)
                                    gp.dma_start(
                                        out=xt[i % 2][:, :],
                                        in_=d_xc[:, i * 4096:(i + 1) * 4096]
                                        ).then_inc(sm[f"s_xt{i % 2}"].h, 16)
                                    sm[f"s_xt{i % 2}"].n += 16
                            for i in range(32):
                                gi = g * 32 + i
                                if i >= 2:
                                    gp.wait_ge(ac.h, acP + 256 + i - 1)
                                elif g > 0:
                                    gp.wait_ge(ac.h, acP)
                                src_ = (d_w1[:, gi * 512:(gi + 1) * 512]
                                        if gi < 64 else
                                        d_w2[:, (gi - 64) * 512:
                                             (gi - 63) * 512])
                                gp.dma_start(out=wstr[i % 2][:, :], in_=src_
                                             ).then_inc(sm[f"s_w1{i % 2}"].h, 16)
                                sm[f"s_w1{i % 2}"].n += 16

                        @blk.tensor
                        def _(t):
                            t.wait_ge(sm["s_gw"].h, gw0 + 80)
                            for rt in range(12):
                                t.wait_ge(sm[f"s_xt{rt % 2}"].h,
                                          xt0[rt % 2] + 16 * (rt // 2 + 1))
                                if rt >= 2:
                                    t.wait_ge(dv.h, rop(rt - 2, 5))
                                elif g > 0:
                                    t.wait_ge(ac.h, acP - 32)
                                w1_, w2_ = (("q1", "q2") if rt < 4 else
                                            ("k1", "k2"))
                                pq = PQ[rt % 2]
                                for dc in range(8):
                                    t.matmul(pq[:, 0:512],
                                             wg[w1_][:, dc * 128:(dc + 1) * 128],
                                             xt[rt % 2][:, dc * 512:(dc + 1) * 512],
                                             start=(dc == 0), stop=(dc == 7))
                                for dc in range(8):
                                    ins = t.matmul(
                                        pq[:, 512:1024],
                                        wg[w2_][:, dc * 128:(dc + 1) * 128],
                                        xt[rt % 2][:, dc * 512:(dc + 1) * 512],
                                        start=(dc == 0), stop=(dc == 7))
                                ins.then_inc(sm["e_pj"].h, 1)
                                sm["e_pj"].n += 1
                                if rt < 4:
                                    continue
                                for sub in range(4):
                                    vi = (rt - 4) * 4 + sub
                                    if vi >= 2:
                                        t.wait_ge(dv.h, vev(vi - 2))
                                    elif g > 0:
                                        t.wait_ge(dv.h, dv_att_end)
                                    pv = (p_m[:, 0:256] if vi % 2 == 0 else
                                          p_m[:, 512:768])
                                    for dc in range(8):
                                        ins = t.matmul(
                                            pv,
                                            xt[rt % 2][:, dc * 512 + sub * 128:
                                                       dc * 512 + sub * 128
                                                       + 128],
                                            wvg[:, dc * 256:(dc + 1) * 256],
                                            start=(dc == 0), stop=(dc == 7))
                                    ins.then_inc(sm["e_pv"].h, 1)
                                    sm["e_pv"].n += 1

                        @blk.vector
                        def _(v):
                            vo = mk_vo(v)
                            if g == 0:
                                v.wait_ge(sm["s_init"].h, initN)
                                v.wait_ge(sm["s_init2"].h, init2N)
                            ones_view = vtok[:, :].rearrange(
                                "p (t h c) -> p t h c", t=32, h=4)[:, :, :, 64:65]
                            vo(lambda: v.memset(ones_view, 1.0))
                            for rt in range(12):
                                v.wait_ge(sm["e_pj"].h, pj0 + rt + 1)
                                if rt >= 4:
                                    v.wait_ge(sm[f"s_rk{rt % 4}"].h,
                                              rk0[rt % 4] + 128 * (rt // 4))
                                elif g > 0:
                                    v.wait_ge(sm[f"s_rk{rt % 4}"].h,
                                              rk0[rt % 4])
                                pq = PQ[rt % 2]
                                toff = ((S + rt * 512) if rt < 4 else
                                        ((rt - 4) * 512))
                                tc = tab_c[:, toff:toff + 512]
                                tsn = tab_s[:, toff:toff + 512]
                                r1, r2 = rt1[rt % 4], rt2[rt % 4]
                                vo(lambda: v.tensor_mul(r1[:, :], pq[:, 0:512],
                                                        tc))
                                vo(lambda: v.tensor_mul(rscr[:, :],
                                                        pq[:, 512:1024], tsn))
                                vo(lambda: v.tensor_sub(r1[:, :], r1[:, :],
                                                        rscr[:, :]))
                                vo(lambda: v.tensor_mul(r2[:, :], pq[:, 0:512],
                                                        tsn))
                                vo(lambda: v.tensor_mul(rscr[:, :],
                                                        pq[:, 512:1024], tc))
                                vo(lambda: v.tensor_add(r2[:, :], r2[:, :],
                                                        rscr[:, :]))
                                if rt >= 4:
                                    for sub in range(4):
                                        vi = (rt - 4) * 4 + sub
                                        v.wait_ge(sm["e_pv"].h, pv0 + vi + 1)
                                        pv = (p_m[:, 0:256] if vi % 2 == 0 else
                                              p_m[:, 512:768])
                                        dst = vtok[:, :].rearrange(
                                            "p (t h c) -> p t h c", t=32, h=4
                                            )[:, vi:vi + 1, :, 0:64]
                                        vo(lambda: v.tensor_copy(dst, pv))
                            assert dv.n == dvP + 1 + 24 + 8 * 10

                        def emit_restack(eng, rts):
                            for rt in rts:
                                eng.wait_ge(dv.h, rop(rt, 6))
                                r1, r2 = rt1[rt % 4], rt2[rt % 4]
                                if rt < 4:
                                    dsts = (qsA, qsB)
                                    off = rt * 512
                                else:
                                    dsts = (ksA, ksB)
                                    off = (rt - 4) * 512
                                for half, dstT in enumerate(dsts):
                                    b0 = half * 64
                                    for hh, srcT in ((0, r1), (32, r2)):
                                        eng.dma_start(
                                            out=dstT[hh:hh + 32, off:off + 512],
                                            in_=srcT[b0:b0 + 32, :]
                                            ).then_inc(sm[f"s_rk{rt % 4}"].h, 16)
                                        sm[f"s_rk{rt % 4}"].n += 16
                                        eng.dma_start(
                                            out=dstT[64 + hh:96 + hh,
                                                     off:off + 512],
                                            in_=srcT[b0 + 32:b0 + 64, :]
                                            ).then_inc(sm[f"s_rk{rt % 4}"].h, 16)
                                        sm[f"s_rk{rt % 4}"].n += 16

                        @blk.scalar
                        def _(sc):
                            if g > 0:
                                sc.wait_ge(sm["e_sc"].h, sm["e_sc"].n)
                            emit_restack(sc, range(1, 12, 2))

                        @blk.sync
                        def _(s):
                            if g > 0:
                                s.wait_ge(sm["e_sc"].h, sm["e_sc"].n)
                            emit_restack(s, range(0, 12, 2))

                    sc0 = sm["e_sc"].n
                    av0 = sm["e_av"].n
                    tp0 = sm["e_tp"].n
                    ac0 = ac.n
                    dvA = dv.n
                    nm2 = lambda u, hb: dvA + u * 6 + 2 * (hb + 1)
                    nrm_done = lambda u: dvA + u * 6 + 4
                    ev_done = lambda u: dvA + u * 6 + 6
                    rkN = [sm[f"s_rk{i}"].n for i in range(4)]
                    if True:
                        @blk.tensor
                        def _(t):
                            for i in range(4):
                                t.wait_ge(sm[f"s_rk{i}"].h, rkN[i])
                            for w in range(34):
                                if 1 <= w <= 32 and w - 1 >= 2:
                                    t.wait_ge(dv.h, nrm_done(w - 3))
                                for ttg in range(8):
                                    if w < 32:
                                        u, pr, sg = w, w % 2, w // 2
                                        qs = qsA if pr == 0 else qsB
                                        ks = ksA if pr == 0 else ksB
                                        j = u * 8 + ttg
                                        if j >= 2:
                                            t.wait_ge(ac.h, ac0 + j - 1)
                                        else:
                                            t.wait_ge(dv.h, rop(10 + j, 5))
                                        pq = PQ[j % 2]
                                        for tq in range(4):
                                            tt = ttg * 4 + tq
                                            for hb in range(2):
                                                ins = t.matmul(
                                                    pq[:, hb * 512 + tq * 128:
                                                       hb * 512 + tq * 128 + 128],
                                                    ks[hb * 64:hb * 64 + 64,
                                                       tt * 128:(tt + 1) * 128],
                                                    qs[hb * 64:hb * 64 + 64,
                                                       sg * 128:(sg + 1) * 128],
                                                    start=(tq == 0),
                                                    stop=(tq == 3))
                                        ins.then_inc(sm["e_sc"].h, 1)
                                        sm["e_sc"].n += 1
                                    if 1 <= w <= 32:
                                        u = w - 1
                                        pr = u % 2
                                        if w == 1 and ttg == 0:
                                            t.wait_ge(dv.h, vev(31))
                                        t.wait_ge(ac.h,
                                                  ac0 + u * 8 + ttg + 1)
                                        pav = (p_m[:, 0:130] if u % 2 == 0 else
                                               p_m[:, 512:642])
                                        for hb in range(2):
                                            for tq in range(4):
                                                tt = ttg * 4 + tq
                                                ins = t.matmul(
                                                    pav[:, hb * 65:hb * 65 + 65],
                                                    slabs[pr][:,
                                                              ttg * 1024 +
                                                              hb * 512 +
                                                              tq * 128:
                                                              ttg * 1024 +
                                                              hb * 512 +
                                                              tq * 128 + 128],
                                                    vtok[:, tt * 260 +
                                                         (pr * 2 + hb) * 65:
                                                         tt * 260 +
                                                         (pr * 2 + hb) * 65
                                                         + 65],
                                                    start=(ttg == 0 and
                                                           hb == 0 and tq == 0),
                                                    stop=(ttg == 7 and
                                                          hb == 1 and tq == 3))
                                        ins.then_inc(sm["e_av"].h, 1)
                                        sm["e_av"].n += 1
                                if 2 <= w:
                                    u = w - 2
                                    if u < 32:
                                        pr, sg = u % 2, u // 2
                                        if u >= 4:
                                            t.wait_ge(dv.h, ev_done(u - 2))
                                        for hb in range(2):
                                            hg = 4 * g + pr * 2 + hb
                                            t.wait_ge(dv.h, nm2(u, hb))
                                            bp = 64 * (hg % 2)
                                            ins = t.transpose(
                                                p_ts[u % 2][bp:bp + 64,
                                                            hb * 128:
                                                            hb * 128 + 128],
                                                nrm[(u % 2) * 2 + hb][:, 0:64],
                                                identb[:, :])
                                        ins.then_inc(sm["e_tp"].h, 1)
                                        sm["e_tp"].n += 1

                        @blk.scalar
                        def _(sc):
                            ao = mk_ao(sc)
                            for u in range(32):
                                pr = u % 2
                                for ttg in range(8):
                                    j = u * 8 + ttg
                                    if u >= 2:
                                        sc.wait_ge(sm["e_av"].h,
                                                   av0 + (u - 2) * 8 + ttg + 1)
                                    elif g > 0 and ttg == 0:
                                        sc.wait_ge(sm["e_av"].h, av0)
                                    sc.wait_ge(sm["e_sc"].h, sc0 + j + 1)
                                    pq = PQ[j % 2]
                                    dst = slabs[pr][:, ttg * 1024:
                                                     ttg * 1024 + 1024]
                                    sc.activation(
                                        dst, pq[:, 0:1024], AF.Exp,
                                        bias=EXPB, scale=SCL
                                        ).then_inc(ac.h, 1)
                                    ac.n += 1
                            assert ac.n == ac0 + 256
                            for i in range(32):
                                gi = g * 32 + i
                                sc.wait_ge(sm[f"s_w1{i % 2}"].h,
                                           w1b[i % 2] + 16 * (i // 2 + 1))
                                ao(lambda: sc.activation(
                                    absd[:, :], wstr[i % 2][:, :], AF.Abs,
                                    accum_out=wsums[:, gi:gi + 1]))
                            assert ac.n == ac0 + 288

                        @blk.vector
                        def _(v):
                            vo = mk_vo(v)
                            for u in range(32):
                                pr, sg = u % 2, u // 2
                                v.wait_ge(sm["e_av"].h, av0 + (u + 1) * 8)
                                pav = (p_m[:, 0:130] if u % 2 == 0 else
                                       p_m[:, 512:642])
                                for hb in range(2):
                                    vo(lambda: v.reciprocal(
                                        rcp[:, 0:1],
                                        pav[:, hb * 65 + 64:hb * 65 + 65]))
                                    vo(lambda: v.tensor_scalar(
                                        nrm[(u % 2) * 2 + hb][:, 0:64],
                                        pav[:, hb * 65:hb * 65 + 64],
                                        rcp[:, 0:1], None, op0=ALU.mult))
                                v.wait_ge(sm["e_tp"].h, tp0 + u + 1)
                                for hb in range(2):
                                    hg = 4 * g + pr * 2 + hb
                                    bp = 64 * (hg % 2)
                                    dco = hg // 2
                                    col = dco * SO + sg * 128
                                    vo(lambda: v.tensor_copy(
                                        attn_o[bp:bp + 64, col:col + 128],
                                        p_ts[u % 2][bp:bp + 64,
                                                    hb * 128:hb * 128 + 128]))
                            assert dv.n == dvA + 32 * 6
                        dv_att_end = dv.n

            w1t = cf.enter_context(nc.sbuf_tensor("w1t", [128, 8 * FF], BF16))
            wst2 = [cf.enter_context(nc.sbuf_tensor(f"wst2_{i}", [128, 512],
                                                    F32)) for i in range(4)]
            w4 = lambda i: f"s_w{1 + (i % 4) // 2}{i % 2}"
            m1b = cf.enter_context(nc.sbuf_tensor("m1b", [128, 512], BF16))
            with contextlib.ExitStack() as cx:
                def sbc(name, shape, dt, n=None):
                    if n is None:
                        return cx.enter_context(
                            nc.sbuf_tensor(name, shape, dt, side="right"))
                    return [cx.enter_context(
                        nc.sbuf_tensor(f"{name}{i}", shape, dt, side="right"))
                        for i in range(n)]

                wo_sb = sbc("wo_sb", [128, 8192], BF16)
                res1 = sbc("res1", [128, 4096], F32)
                res1b = sbc("res1b", [128, 4096], BF16)
                sqb = sbc("sqb", [128, 4096], BF16)
                xres = sbc("xres", [128, 512], F32, 2)
                scr = sbc("scr", [128, 512], F32)
                rw_m = sbc("rw_m", [1, 512], F32)
                rw_s = sbc("rw_s", [1, 512], F32)
                rw_t = sbc("rw_t", [1, 512], F32)
                rw_d = sbc("rw_d", [1, 512], F32)
                rw_r = sbc("rw_r", [1, 512], F32)
                p_p = [cx.enter_context(nc.psum_tensor(f"p_p{i}", [128, 512],
                                                       F32)) for i in range(2)]
                p_me = cx.enter_context(nc.psum_tensor("p_me", [1, 512], F32))
                p_sq = cx.enter_context(nc.psum_tensor("p_sq", [1, 512], F32))
                p_bc = cx.enter_context(nc.psum_tensor("p_bc", [128, 1024], F32))

                cm0 = sm["e_cm"].n
                csm0 = sm["e_csm"].n
                cbc0 = sm["e_cbc"].n
                dvC = dv.n
                acC = ac.n
                cev = lambda i: dvC + 6 + (i // 8) * 77 + (i % 8) * 7 + 2
                csq = lambda i: dvC + 6 + (i // 8) * 77 + (i % 8) * 7 + 3
                ctn = lambda t: (dvC + 6 + ((t // 2) // 8) * 77 +
                                 ((t // 2) % 8) * 7 + (5 if t % 2 == 0 else 7))
                crw = lambda st: dvC + 6 + st * 77 + 60
                crc = lambda st: dvC + 6 + st * 77 + 61
                cx1 = lambda st: dvC + 6 + st * 77 + 77
                csd = lambda st: acC + st + 1
                res0 = [sm["s_res0"].n, sm["s_res1"].n]

                w1c = [sm["s_w10"].n, sm["s_w11"].n,
                       sm["s_w20"].n, sm["s_w21"].n]
                with nc.Block() as blk:
                    @blk.gpsimd
                    def _(gp):
                        gp.dma_start(out=wo_sb[:, :], in_=d_wo
                                     ).then_inc(sm["s_wo"].h, 16)
                        sm["s_wo"].n += 16
                        for i in range(64):
                            if i >= 4:
                                gp.wait_ge(dv.h, ctn(i - 4))
                            gp.dma_start(
                                out=wst2[i % 4][:, :],
                                in_=d_w1[:, i * 512:(i + 1) * 512]
                                ).then_inc(sm[w4(i)].h, 16)
                            sm[w4(i)].n += 16

                    @blk.sync
                    def _(s):
                        for i in range(32):
                            st, ec = i // 8, i % 8
                            if i >= 2:
                                s.wait_ge(dv.h, cev(i - 2) - 1)
                            s.dma_start(out=xres[i % 2][:, :],
                                        in_=d_xq[:, st * 4096 + ec * 512:
                                                 st * 4096 + ec * 512 + 512]
                                        ).then_inc(sm[f"s_res{i % 2}"].h, 16)
                            sm[f"s_res{i % 2}"].n += 16

                    @blk.tensor
                    def _(t):
                        t.matmul(p_me[0:1, 0:128], ones_f[:, 0:1],
                                 wsums[:, 0:128],
                                 start=True, stop=True).then_inc(sm["e_csm"].h, 1)
                        sm["e_csm"].n += 1
                        t.wait_ge(dv.h, dvC + 4)
                        t.matmul(p_bc[:, 0:2], ones_f[0:1, 0:128],
                                 trow[0:1, 0:2], start=True,
                                 stop=True).then_inc(sm["e_cbc"].h, 1)
                        sm["e_cbc"].n += 1
                        t.wait_ge(sm["s_wo"].h, sm["s_wo"].n)
                        for st in range(4):
                            if st >= 1:
                                t.wait_ge(dv.h, cx1(st - 1))
                            for ec in range(8):
                                i = st * 8 + ec
                                if i >= 2:
                                    t.wait_ge(dv.h, cev(i - 2))
                                for dc in range(8):
                                    ins = t.matmul(
                                        p_p[i % 2][:, 0:512],
                                        wo_sb[:, dc * 1024 + ec * 128:
                                              dc * 1024 + ec * 128 + 128],
                                        attn_o[:, dc * SO + st * 512:
                                               dc * SO + st * 512 + 512],
                                        start=(dc == 0), stop=(dc == 7))
                                ins.then_inc(sm["e_cm"].h, 1)
                                sm["e_cm"].n += 1
                            for ec in range(8):
                                t.wait_ge(dv.h, cev(st * 8 + ec))
                                t.matmul(p_me[0:1, :], ones_b[:, 0:1],
                                         res1b[:, ec * 512:(ec + 1) * 512],
                                         start=(ec == 0), stop=(ec == 7))
                            for ec in range(8):
                                t.wait_ge(dv.h, csq(st * 8 + ec))
                                ins = t.matmul(p_sq[0:1, :], ones_b[:, 0:1],
                                               sqb[:, ec * 512:(ec + 1) * 512],
                                               start=(ec == 0), stop=(ec == 7))
                            ins.then_inc(sm["e_csm"].h, 1)
                            sm["e_csm"].n += 1
                            t.wait_ge(dv.h, crc(st))
                            t.matmul(p_bc[:, 0:512], ones_f[0:1, 0:128],
                                     rw_m[0:1, :], start=True, stop=True)
                            t.matmul(p_bc[:, 512:1024], ones_f[0:1, 0:128],
                                     rw_r[0:1, :], start=True,
                                     stop=True).then_inc(sm["e_cbc"].h, 1)
                            sm["e_cbc"].n += 1

                    @blk.vector
                    def _(v):
                        vo = mk_vo(v)
                        v.wait_ge(sm["e_csm"].h, csm0 + 1)
                        vo(lambda: v.tensor_reduce(trow[0:1, 0:1],
                                                   p_me[0:1, 0:64], AX.X,
                                                   ALU.add))
                        vo(lambda: v.tensor_reduce(trow[0:1, 1:2],
                                                   p_me[0:1, 64:128], AX.X,
                                                   ALU.add))
                        vo(lambda: v.tensor_scalar(trow[0:1, 0:2],
                                                   trow[0:1, 0:2], 0.5 / NW,
                                                   None, op0=ALU.mult))
                        vo(lambda: v.tensor_scalar(trow[0:1, 0:2],
                                                   trow[0:1, 0:2], 0.5e-8,
                                                   None, op0=ALU.max))
                        v.wait_ge(sm["e_cbc"].h, cbc0 + 1)
                        vo(lambda: v.tensor_copy(thr[:, :], p_bc[:, 0:2]))
                        vo(lambda: v.tensor_scalar(thrn[:, :], p_bc[:, 0:2],
                                                   -1.0, None, op0=ALU.mult))
                        for st in range(4):
                            for ec in range(8):
                                i = st * 8 + ec
                                v.wait_ge(sm["e_cm"].h, cm0 + i + 1)
                                v.wait_ge(sm[f"s_res{i % 2}"].h,
                                          res0[i % 2] + 16 * (i // 2 + 1))
                                vo(lambda: v.scalar_tensor_tensor(
                                    res1[:, ec * 512:(ec + 1) * 512],
                                    p_p[i % 2][:, 0:512], sb_bo[:, ec:ec + 1],
                                    xres[i % 2][:, :],
                                    op0=ALU.add, op1=ALU.add))
                                vo(lambda: v.tensor_copy(
                                    res1b[:, ec * 512:(ec + 1) * 512],
                                    res1[:, ec * 512:(ec + 1) * 512]))
                                vo(lambda: v.tensor_mul(
                                    sqb[:, ec * 512:(ec + 1) * 512],
                                    res1b[:, ec * 512:(ec + 1) * 512],
                                    res1b[:, ec * 512:(ec + 1) * 512]))
                                for tt_ in (2 * i, 2 * i + 1):
                                    v.wait_ge(sm[w4(tt_)].h,
                                              w1c[tt_ % 4] +
                                              16 * (tt_ // 4 + 1))
                                    vo(lambda: v.tensor_scalar(
                                        m1b[:, :], wst2[tt_ % 4][:, :],
                                        thrn[:, 0:1], None, op0=ALU.is_le))
                                    vo(lambda: v.scalar_tensor_tensor(
                                        w1t[:, tt_ * 512:(tt_ + 1) * 512],
                                        wst2[tt_ % 4][:, :], thr[:, 0:1],
                                        m1b[:, :], op0=ALU.is_ge,
                                        op1=ALU.subtract))
                            v.wait_ge(sm["e_csm"].h, csm0 + st + 2)
                            vo(lambda: v.tensor_scalar(rw_m[0:1, :],
                                                       p_me[0:1, :],
                                                       1.0 / 1024.0, None,
                                                       op0=ALU.mult))
                            vo(lambda: v.tensor_scalar(rw_s[0:1, :],
                                                       p_sq[0:1, :],
                                                       1.0 / 1024.0, None,
                                                       op0=ALU.mult))
                            vo(lambda: v.tensor_mul(rw_t[0:1, :], rw_m[0:1, :],
                                                    rw_m[0:1, :]))
                            vo(lambda: v.tensor_sub(rw_s[0:1, :], rw_s[0:1, :],
                                                    rw_t[0:1, :]))
                            v.wait_ge(ac.h, csd(st))
                            vo(lambda: v.reciprocal(rw_r[0:1, :], rw_d[0:1, :]))
                            v.wait_ge(sm["e_cbc"].h, cbc0 + st + 2)
                            for ec in range(8):
                                vo(lambda: v.tensor_sub(
                                    scr[:, :],
                                    res1[:, ec * 512:(ec + 1) * 512],
                                    p_bc[:, 0:512]))
                                vo(lambda: v.tensor_mul(
                                    x1[:, ec * SO + st * 512:
                                       ec * SO + st * 512 + 512],
                                    scr[:, :], p_bc[:, 512:1024]))
                        assert dv.n == dvC + 6 + 4 * 77

                    @blk.scalar
                    def _(sc):
                        ao = mk_ao(sc)
                        for st in range(4):
                            sc.wait_ge(dv.h, crw(st))
                            ao(lambda: sc.activation(rw_d[0:1, :], rw_s[0:1, :],
                                                     AF.Sqrt, bias=EPS,
                                                     scale=1.0))
                        assert ac.n == acC + 4

        acx.close()

        with contextlib.ExitStack() as ds:
            def sbd(name, shape, dt, n=None):
                if n is None:
                    return ds.enter_context(
                        nc.sbuf_tensor(name, shape, dt, side="right"))
                return [ds.enter_context(
                    nc.sbuf_tensor(f"{name}{i}", shape, dt, side="right"))
                    for i in range(n)]

            w2t = sbd("w2t", [128, 32 * 1024], BF16)
            h_sb = sbd("h_sb", [128, 32 * 256], BF16)
            res2 = sbd("res2", [128, 8 * 256], F32)
            sq2 = sbd("sq2_", [128, 256], BF16, 2)
            outt = sbd("outt", [128, 256], F32, 2)
            dw_m = sbd("dw_m", [1, 256], F32)
            dw_s = sbd("dw_s", [1, 256], F32)
            dw_t = sbd("dw_t", [1, 256], F32)
            dw_d = sbd("dw_d", [1, 256], F32)
            dw_r = sbd("dw_r", [1, 256], F32)
            p_h = ds.enter_context(nc.psum_tensor("p_h", [128, 1024], F32))
            p_o = ds.enter_context(nc.psum_tensor("p_o", [128, 1024], F32))
            p_m2 = ds.enter_context(nc.psum_tensor("p_m2", [1, 256], F32))
            p_q2 = ds.enter_context(nc.psum_tensor("p_q2", [1, 256], F32))
            p_b2 = ds.enter_context(nc.psum_tensor("p_b2", [128, 512], F32))

            dh0 = sm["e_dh"].n
            do0 = sm["e_do"].n
            dsm0 = sm["e_dsm"].n
            dbc0 = sm["e_dbc"].n
            dvF = dv.n
            acF = ac.n
            w2c = [sm["s_w10"].n, sm["s_w11"].n,
                   sm["s_w20"].n, sm["s_w21"].n]
            gpT = sm["e_gp"].n
            w20 = [sm["s_w20"].n, sm["s_w21"].n]
            dev = lambda i: dvF + 128 + (i // 8) * 37 + (i % 8) * 2 + 1
            dsq = lambda i: dvF + 128 + (i // 8) * 37 + (i % 8) * 2 + 2
            drw = lambda st: dvF + 128 + st * 37 + 20
            drc = lambda st: dvF + 128 + st * 37 + 21
            dout = lambda i: dvF + 128 + (i // 8) * 37 + 21 + 2 * ((i % 8) + 1)
            dall = lambda st: dvF + 128 + st * 37 + 37
            dgel = lambda i: acF + (i // 32) * 33 + (i % 32) + 1
            dsqr = lambda st: acF + st * 33 + 33
            st0 = [sm["s_st0"].n, sm["s_st1"].n]
            with nc.Block() as blk:
                @blk.tensor
                def _(t):
                    for st in range(8):
                        for fc in range(32):
                            i = st * 32 + fc
                            if i >= 2:
                                t.wait_ge(ac.h, dgel(i - 2))
                            ph = (p_h[:, 0:256] if fc % 2 == 0 else
                                  p_h[:, 512:768])
                            for dc in range(8):
                                ins = t.matmul(
                                    ph,
                                    w1t[:, dc * FF + fc * 128:
                                        dc * FF + fc * 128 + 128],
                                    x1[:, dc * SO + st * 256:
                                       dc * SO + st * 256 + 256],
                                    start=(dc == 0), stop=(dc == 7))
                            ins.then_inc(sm["e_dh"].h, 1)
                            sm["e_dh"].n += 1
                        t.wait_ge(ac.h, dgel(st * 32 + 31))

                        def stats(dc):
                            if st >= 1 and dc == 0:
                                t.wait_ge(dv.h, drw(st - 1))
                            t.wait_ge(dv.h, dev(st * 8 + dc))
                            t.matmul(p_m2[0:1, :], ones_f[:, 0:1],
                                     res2[:, dc * 256:(dc + 1) * 256],
                                     start=(dc == 0), stop=(dc == 7))
                            t.wait_ge(dv.h, dsq(st * 8 + dc))
                            ins = t.matmul(p_q2[0:1, :], ones_b[:, 0:1],
                                           sq2[dc % 2][:, :],
                                           start=(dc == 0), stop=(dc == 7))
                            if dc == 7:
                                ins.then_inc(sm["e_dsm"].h, 1)
                                sm["e_dsm"].n += 1

                        for dc in range(8):
                            i = st * 8 + dc
                            if st == 0:
                                t.wait_ge(dv.h, dvF + (dc // 4 + 1) * 64)
                            if i >= 2:
                                t.wait_ge(dv.h, dev(i - 2))
                            po = (p_o[:, 0:256] if dc % 2 == 0 else
                                  p_o[:, 512:768])
                            for fc in range(32):
                                ins = t.matmul(
                                    po,
                                    w2t[:, fc * 1024 + dc * 128:
                                        fc * 1024 + dc * 128 + 128],
                                    h_sb[:, fc * 256:(fc + 1) * 256],
                                    start=(fc == 0), stop=(fc == 31))
                            ins.then_inc(sm["e_do"].h, 1)
                            sm["e_do"].n += 1
                            if dc >= 1:
                                stats(dc - 1)
                        stats(7)
                        t.wait_ge(dv.h, drc(st))
                        if st >= 1:
                            t.wait_ge(dv.h, dall(st - 1))
                        t.matmul(p_b2[:, 0:256], ones_f[0:1, 0:128],
                                 dw_m[0:1, :], start=True, stop=False)
                        t.matmul(p_b2[:, 256:512], ones_f[0:1, 0:128],
                                 dw_r[0:1, :], start=False, stop=True
                                 ).then_inc(sm["e_dbc"].h, 1)
                        sm["e_dbc"].n += 1

                @blk.gpsimd
                def _(gp):
                    for i in range(64):
                        half, fc = i // 32, i % 32
                        if i >= 4:
                            gp.wait_ge(dv.h, dvF + 2 * (i - 4) + 2)
                        gp.dma_start(
                            out=wst2[i % 4][:, :],
                            in_=d_w2[:, fc * 1024 + half * 512:
                                     fc * 1024 + half * 512 + 512]
                            ).then_inc(sm[w4(i)].h, 16)
                        sm[w4(i)].n += 16

                @blk.scalar
                def _(sc):
                    ao = mk_ao(sc)
                    for st in range(8):
                        if st >= 1:
                            sc.wait_ge(sm["e_do"].h, do0 + st * 8)
                        for fc in range(32):
                            i = st * 32 + fc
                            sc.wait_ge(sm["e_dh"].h, dh0 + i + 1)
                            ph = (p_h[:, 0:256] if fc % 2 == 0 else
                                  p_h[:, 512:768])
                            sc.activation(
                                h_sb[:, fc * 256:(fc + 1) * 256], ph, AF.Gelu,
                                bias=sb_b1[:, fc:fc + 1], scale=1.0
                                ).then_inc(ac.h, 1)
                            ac.n += 1
                        sc.wait_ge(dv.h, drw(st))
                        ao(lambda: sc.activation(dw_d[0:1, :], dw_s[0:1, :],
                                                 AF.Sqrt, bias=EPS, scale=1.0))
                    assert ac.n == acF + 8 * 33

                @blk.vector
                def _(v):
                    vo = mk_vo(v)
                    for t_ in range(64):
                        halft, fct = t_ // 32, t_ % 32
                        v.wait_ge(sm[w4(t_)].h,
                                  w2c[t_ % 4] + 16 * (t_ // 4 + 1))
                        vo(lambda: v.tensor_scalar(
                            m1b[:, :], wst2[t_ % 4][:, :], thrn[:, 1:2],
                            None, op0=ALU.is_le))
                        vo(lambda: v.scalar_tensor_tensor(
                            w2t[:, fct * 1024 + halft * 512:
                                fct * 1024 + halft * 512 + 512],
                            wst2[t_ % 4][:, :], thr[:, 1:2], m1b[:, :],
                            op0=ALU.is_ge, op1=ALU.subtract))
                    for st in range(8):
                        for dc in range(8):
                            i = st * 8 + dc
                            v.wait_ge(sm["e_do"].h, do0 + i + 1)
                            po = (p_o[:, 0:256] if dc % 2 == 0 else
                                  p_o[:, 512:768])
                            vo(lambda: v.scalar_tensor_tensor(
                                res2[:, dc * 256:(dc + 1) * 256], po,
                                sb_b2[:, dc:dc + 1],
                                x1[:, dc * SO + st * 256:
                                   dc * SO + st * 256 + 256],
                                op0=ALU.add, op1=ALU.add))
                            vo(lambda: v.tensor_mul(
                                sq2[dc % 2][:, :],
                                res2[:, dc * 256:(dc + 1) * 256],
                                res2[:, dc * 256:(dc + 1) * 256]))
                        v.wait_ge(sm["e_dsm"].h, dsm0 + st + 1)
                        vo(lambda: v.tensor_scalar(dw_m[0:1, :], p_m2[0:1, :],
                                                   1.0 / 1024.0, None,
                                                   op0=ALU.mult))
                        vo(lambda: v.tensor_scalar(dw_s[0:1, :], p_q2[0:1, :],
                                                   1.0 / 1024.0, None,
                                                   op0=ALU.mult))
                        vo(lambda: v.tensor_mul(dw_t[0:1, :], dw_m[0:1, :],
                                                dw_m[0:1, :]))
                        vo(lambda: v.tensor_sub(dw_s[0:1, :], dw_s[0:1, :],
                                                dw_t[0:1, :]))
                        v.wait_ge(ac.h, dsqr(st))
                        vo(lambda: v.reciprocal(dw_r[0:1, :], dw_d[0:1, :]))
                        v.wait_ge(sm["e_dbc"].h, dbc0 + st + 1)
                        for ec in range(8):
                            i = st * 8 + ec
                            if i >= 2:
                                v.wait_ge(sm[f"s_st{i % 2}"].h,
                                          st0[i % 2] + 16 * (i // 2))
                            vo(lambda: v.tensor_sub(
                                outt[i % 2][:, :],
                                res2[:, ec * 256:(ec + 1) * 256],
                                p_b2[:, 0:256]))
                            vo(lambda: v.tensor_mul(outt[i % 2][:, :],
                                                    outt[i % 2][:, :],
                                                    p_b2[:, 256:512]))
                    assert dv.n == dvF + 128 + 8 * 37

                @blk.sync
                def _(s):
                    for st in range(8):
                        for ec in range(8):
                            i = st * 8 + ec
                            s.wait_ge(dv.h, dout(i))
                            s.dma_start(out=d_out[:, st * 2048 + ec * 256:
                                                  st * 2048 + ec * 256 + 256],
                                        in_=outt[i % 2][:, :]
                                        ).then_inc(sm[f"s_st{i % 2}"].h, 16)
                            sm[f"s_st{i % 2}"].n += 16

        cf.close()

        with nc.Block() as blk:
            @blk.gpsimd
            def _(g):
                g.wait_ge(sm["s_st0"].h, sm["s_st0"].n)
                g.wait_ge(sm["s_st1"].h, sm["s_st1"].n)

    return nc



def _f32c(a):
    return np.ascontiguousarray(a, dtype=np.float32)


def _feat_major(xb):
    T = xb.shape[0]
    t = xb.reshape(T // 512, 512, 8, 128)
    t = t.transpose(3, 0, 2, 1)
    return _f32c(t.reshape(128, T * 8))


def _wsel(w, half):
    r = w.reshape(16, 2, 32, 1024)[:, half]
    r = r.reshape(4, 128, 8, 128)
    r = r.transpose(3, 0, 2, 1)
    return _f32c(r.reshape(128, 4096))


def _wv_prep(wv):
    r = wv.reshape(4, 256, 8, 128).transpose(3, 0, 2, 1)
    return _f32c(r.reshape(128, 8192))


def _wo_prep(wo):
    r = wo.reshape(1024, 8, 128).transpose(2, 1, 0)
    return _f32c(r.reshape(128, 8192))


def _w1_prep(w1):
    r = w1.reshape(4096, 8, 128).transpose(2, 1, 0)
    return _f32c(r.reshape(128, 8 * 4096))


def _w2_prep(w2):
    r = w2.reshape(1024, 32, 128).transpose(2, 1, 0)
    return _f32c(r.reshape(128, 32 * 1024))


def _rope_tables():
    inv = (10000.0 ** (-(np.arange(0, HD, 2, dtype=np.float32) / HD))
           ).astype(np.float32)
    t = np.arange(S, dtype=np.float32)
    fr = t[:, None] * inv[None, :]
    cos = np.cos(fr).astype(np.float32)
    sin = np.sin(fr).astype(np.float32)
    return _f32c(np.tile(cos.T, (4, 1))), _f32c(np.tile(sin.T, (4, 1)))


def make_in_maps(L0_attn_x, L0_attn_w_qkv, L0_attn_w_out, L0_attn_b_out,
                 L0_ffn_W1, L0_ffn_b1, L0_ffn_W2, L0_ffn_b2):
    x = np.asarray(L0_attn_x, dtype=np.float32)
    wqkv = np.asarray(L0_attn_w_qkv, dtype=np.float32)
    wo = np.asarray(L0_attn_w_out, dtype=np.float32)
    bo = np.asarray(L0_attn_b_out, dtype=np.float32)
    W1 = np.asarray(L0_ffn_W1, dtype=np.float32)
    b1 = np.asarray(L0_ffn_b1, dtype=np.float32)
    W2 = np.asarray(L0_ffn_W2, dtype=np.float32)
    b2 = np.asarray(L0_ffn_b2, dtype=np.float32)

    wq, wk, wv = wqkv[0:D], wqkv[D:2 * D], wqkv[2 * D:3 * D]
    ck, sk = _rope_tables()

    shared = {
        "wq1": _wsel(wq, 0), "wq2": _wsel(wq, 1),
        "wk1": _wsel(wk, 0), "wk2": _wsel(wk, 1),
        "wv": _wv_prep(wv), "wo": _wo_prep(wo),
        "w1f": _w1_prep(W1), "w2f": _w2_prep(W2),
        "cos_k": ck, "sin_k": sk,
        "b_out": _f32c(bo.reshape(8, 128).T),
        "b1": _f32c(b1.reshape(32, 128).T),
        "b2": _f32c(b2.reshape(8, 128).T),
        "ident": np.eye(128, dtype=np.float32),
    }

    in_maps = []
    for c in range(8):
        b, half = c // 2, c % 2
        s0 = half * SO
        m = dict(shared)
        m["x_src"] = _feat_major(x[b])
        m["xq_src"] = _feat_major(x[b, s0:s0 + SO])
        m["cos_q"] = _f32c(ck[:, s0:s0 + SO])
        m["sin_q"] = _f32c(sk[:, s0:s0 + SO])
        in_maps.append(m)
    return in_maps


def assemble_out(results):
    out = np.empty((B, S, D), dtype=np.float32)
    for c in range(8):
        b, half = c // 2, c % 2
        o = results[c]["outT"].reshape(128, 8, 8, 256)
        o = o.transpose(1, 3, 2, 0).reshape(SO, D)
        out[b, half * SO:(half + 1) * SO, :] = o
    return out


def kernel(**inputs):
    in_maps = make_in_maps(**inputs)
    nc = build_program()
    res = run_bass_kernel_spmd(nc, in_maps, core_ids=list(range(8)))
    return assemble_out(res.results)
